# version 04
# baseline (speedup 1.0000x reference)
import os
import math
import numpy as np
import ml_dtypes
import concourse.bass as bass
import concourse.mybir as mybir
from concourse.bass_utils import run_bass_kernel_spmd

F32 = mybir.dt.float32
BF16 = mybir.dt.bfloat16
I32 = mybir.dt.int32
AF = mybir.ActivationFunctionType
ALU = mybir.AluOpType
AX = mybir.AxisListType

D, T, IN = 2048, 2048, 8192
SEG = 256
_STAGE = int(os.environ.get("MK_STAGE", "4"))
_PARANOID = bool(int(os.environ.get("MK_PARANOID", "0")))


def _split_const(v, hi_bits):
    h = np.float32(v)
    u = np.uint32(int(h.view(np.uint32)) & ~((1 << (23 - hi_bits)) - 1))
    hi = u.view(np.float32)
    lo = np.float32(np.float64(v) - np.float64(hi))
    return float(hi), float(lo)


C_HI, C_LO = _split_const(-math.log(10000.0) / 64.0, 11)
TPI_HI, TPI_LO = _split_const(2.0 * math.pi, 10)
PIH_HI, PIH_LO = _split_const(math.pi / 2.0, 10)
INV_2PI = float(np.float32(1.0 / (2.0 * math.pi)))
PI_F32 = float(np.float32(math.pi))
MASK_NEG = -30000.0


class Prog:
    ENGINES = ("pe", "act", "dve", "pool", "sp")

    def __init__(self, nc, ctx):
        self.nc = nc
        self.stream = {e: [] for e in self.ENGINES}
        self.cnt = {}
        self.semh = {}
        for s in ("ldp", "lds", "pe", "act", "dve", "pool"):
            self.semh[s] = ctx.enter_context(nc.semaphore(s))
            self.cnt[s] = 0
        self._barrier = None
        self._bdone = set()

    def _add(self, eng, fn, waits, inc, amt):
        w = [x for x in waits if x is not None]
        if self._barrier is not None and eng not in self._bdone:
            w = list(self._barrier.items()) + w
            self._bdone.add(eng)
        if _PARANOID and eng in ("pe", "act", "dve", "pool") and inc == eng:
            w = [(eng, self.cnt[eng])] + w
        evt = None
        if inc is not None:
            self.cnt[inc] += amt
            evt = (inc, self.cnt[inc])
        self.stream[eng].append((fn, w, (inc, amt) if inc else None))
        return evt

    def c(self, eng, fn, waits=(), inc=True):
        return self._add(eng, fn, waits, eng if inc else None, 1)

    def dma(self, queue, dst, src, waits=(), **kw):
        eng = "pool" if queue == "g" else "sp"
        sem = "ldp" if queue == "g" else "lds"

        def fn(e, dst=dst, src=src, kw=kw):
            return e.dma_start(dst, src, **kw)
        return self._add(eng, fn, waits, sem, 16)

    def barrier(self):
        self._barrier = dict(self.cnt)
        self._bdone = set()

    def final_wait(self):
        waits = list(self.cnt.items())
        self._add("sp", lambda e: e.nop(), waits, None, 0)

    def emit(self):
        nc = self.nc
        eng_of = {"pe": "tensor", "act": "scalar", "dve": "vector",
                  "pool": "gpsimd", "sp": "sync"}
        with nc.Block() as blk:
            for ename in self.ENGINES:
                ops = self.stream[ename]
                if not ops:
                    continue

                def runner(e, ops=ops, self=self):
                    waited = {}
                    for fn, waits, inc in ops:
                        need = {}
                        for sem, val in waits:
                            if val > need.get(sem, 0):
                                need[sem] = val
                        for sem, val in need.items():
                            if waited.get(sem, 0) >= val:
                                continue
                            e.wait_ge(self.semh[sem], val)
                            waited[sem] = val
                        ins = fn(e)
                        if inc is not None:
                            sem, amt = inc
                            ins.then_inc(self.semh[sem], amt)
                getattr(blk, eng_of[ename])(runner)


def sylvester(n):
    H = np.array([[1.0]], dtype=np.float32)
    while H.shape[0] < n:
        H = np.block([[H, H], [H, -H]])
    return H


def build_program(stage=_STAGE):
    from contextlib import ExitStack
    ctx = ExitStack()
    nc = bass.Bass("TRN2", target_bir_lowering=False,
                   detect_race_conditions=_PARANOID)
    dt = nc.dram_tensor

    x0 = dt("x0", [D, T], F32, kind="ExternalInput")
    y = dt("y", [D, T], F32, kind="ExternalOutput")
    wqTD = dt("wqT", [16, 16, 128, 128], F32, kind="ExternalInput")
    wkTD = dt("wkT", [16, 16, 128, 128], F32, kind="ExternalInput")
    wvrD = dt("wvr", [16, 128, D], F32, kind="ExternalInput")
    woTD = dt("woT", [16, 16, 128, 128], F32, kind="ExternalInput")
    wqkvD = dt("wqkv", [16, 128, 3 * D], F32, kind="ExternalInput")
    woutTD = dt("woutT", [16, 16, 128, 128], F32, kind="ExternalInput")
    wupTD = dt("wupT", [64, 16, 128, 128], F32, kind="ExternalInput")
    wgTD = dt("wgT", [64, 16, 128, 128], F32, kind="ExternalInput")
    wdnTD = dt("wdnT", [16, 64, 128, 128], F32, kind="ExternalInput")
    su_ugD = dt("su_ug", [128, 16], F32, kind="ExternalInput")
    su_dpD = dt("su_dp", [128, 64], F32, kind="ExternalInput")
    wsuD = dt("wsu", [128, 64], F32, kind="ExternalInput")
    wsgD = dt("wsg", [128, 64], F32, kind="ExternalInput")
    wsdD = dt("wsd", [128, 16], F32, kind="ExternalInput")
    lnscD = dt("lnsc", [128, 1], F32, kind="ExternalInput")
    lnbsD = dt("lnbs", [128, 1], F32, kind="ExternalInput")
    h128D = dt("h128", [128, 128], BF16, kind="ExternalInput")
    identbD = dt("identb", [128, 128], BF16, kind="ExternalInput")
    identfD = dt("identf", [128, 128], F32, kind="ExternalInput")
    maskAD = dt("maskA", [128, 128], F32, kind="ExternalInput")
    maskTD = dt("maskT", [128, 128], BF16, kind="ExternalInput")
    onescolD = dt("onescol", [128, 1], BF16, kind="ExternalInput")
    onesrowD = dt("onesrow", [1, 128], BF16, kind="ExternalInput")
    onesrow32D = dt("onesrow32", [1, 128], F32, kind="ExternalInput")
    iota_tD = dt("iota_t", [1, T], F32, kind="ExternalInput")
    iota_jD = dt("iota_j", [128, 1], F32, kind="ExternalInput")
    iota2dD = dt("iota2d", [64, T], F32, kind="ExternalInput")
    k1tabD = dt("k1tab", [64, T], F32, kind="ExternalInput")
    k2tabD = dt("k2tab", [64, T], F32, kind="ExternalInput")

    xs1 = dt("xs1", [D, T], F32)
    xs2 = dt("xs2", [D, T], F32)
    xs3 = dt("xs3", [D, T], F32)
    qf_s = dt("qf_s", [D, T], BF16)
    kf_s = dt("kf_s", [D, T], BF16)
    vt_s = dt("vt_s", [T, D], BF16)
    ao_s = dt("ao_s", [D, T], BF16)
    qkv_s = dt("qkv_s", [T, 3 * D], BF16)
    dbg1 = dt("dbg1", [128, 4096], BF16, kind="ExternalOutput") if os.environ.get("MK_DBG") else None
    dbg2 = dt("dbg2", [128, 16384], BF16, kind="ExternalOutput") if os.environ.get("MK_DBG") else None

    def sbuf(name, shape, dtp):
        return ctx.enter_context(nc.sbuf_tensor(name, shape, dtp))

    AR1 = sbuf("AR1", [128, 65536], BF16)
    AR2 = sbuf("AR2", [128, 16384], BF16)
    AR3 = sbuf("AR3", [128, 8192], BF16)
    AR4 = sbuf("AR4", [128, 4096], F32)
    CON = sbuf("CON", [128, 2688], BF16)
    CONF = sbuf("CONF", [128, 640], F32)
    ROWS = sbuf("ROWS", [128, 1280], F32)

    PQ = [ctx.enter_context(nc.psum_tensor(f"pq{i}", [128, 512], F32))
          for i in range(6)]
    PMB = ctx.enter_context(nc.psum_tensor("pmb", [128, 1024], BF16))
    PMB2 = ctx.enter_context(nc.psum_tensor("pmb2", [128, 1024], BF16))

    p = Prog(nc, ctx)

    c_h128 = CON[:, 0:128]
    c_ident = CON[:, 128:256]
    c_maskT = CON[:, 256:384]
    c_onescol = CON[:, 384:385]
    c_onesrow = CON[0:1, 512:640]
    c_ctab = CON[:, 640:1664]
    c_stab = CON[:, 1664:2688]
    c_identf = CONF[:, 0:128]
    c_maskA = CONF[:, 128:256]
    c_onesrow32 = CONF[0:1, 256:384]
    c_su_ug = CONF[:, 384:400]
    c_sup = CONF[:, 400:464]
    c_sgt = CONF[:, 464:528]
    c_sdn = CONF[:, 528:544]
    c_lnsc = CONF[:, 544:545]
    c_lnbs = CONF[:, 545:546]
    c_rs128 = CONF[:, 546:547]
    c_iota_p = CONF[:, 547:548]
    c_scr0 = CONF[:, 548:549]
    c_scr1 = CONF[:, 549:550]
    c_n1 = CONF[0:1, 550:551]
    c_n2 = CONF[0:1, 551:552]
    c_n3 = CONF[0:1, 552:553]
    c_su_dpv = CONF[:, 553:617]
    c_eps5 = CONF[:, 617:618]
    c_eps8 = CONF[:, 618:619]
    c_r2048c = CONF[:, 619:620]
    c_r8192c = CONF[:, 620:621]
    c_scrS = CONF[:, 621:622]
    c_scrE = CONF[:, 622:623]
    c_invf64 = CONF[0:64, 622:623]
    nc.const_aps.register(F32, 1e-5, c_eps5)
    nc.const_aps.register(F32, 1e-8, c_eps8)

    r_mb = ROWS[:, 0:256]
    r_rb = ROWS[:, 256:512]
    r_mrow = ROWS[0:1, 512:768]
    r_e2row = ROWS[0:1, 768:1024]
    r_tmprow = ROWS[0:1, 1024:1280]
    a_denr = ROWS[0:1, 512:1024]

    def h_ap(ch):
        return AR1[:, ch * 2048:(ch + 1) * 2048]

    def vb_ap(ch):
        return AR1[:, 32768 + ch * 2048:32768 + (ch + 1) * 2048]

    k_ap = h_ap
    ao_ap = h_ap

    def za_ap(ic):
        return AR1[:, ic * 1024:(ic + 1) * 1024]

    def u_ap(ch):
        return AR2[:, ch * 1024:(ch + 1) * 1024]

    def prologue():
        LV = int(os.environ.get("MK_PRO", "9"))
        cl = []
        for dst, src in [
            (c_h128, h128D.ap()), (c_ident, identbD.ap()), (c_maskT, maskTD.ap()),
            (c_onescol, onescolD.ap()), (c_onesrow, onesrowD.ap()),
            (c_identf, identfD.ap()), (c_maskA, maskAD.ap()),
            (c_onesrow32, onesrow32D.ap()), (c_su_ug, su_ugD.ap()),
            (c_su_dpv, su_dpD.ap()), (c_sup, wsuD.ap()), (c_sgt, wsgD.ap()),
            (c_sdn, wsdD.ap()), (c_lnsc, lnscD.ap()), (c_lnbs, lnbsD.ap()),
            (c_iota_p, iota_jD.ap()),
        ]:
            cl.append(p.dma("g", dst, src))
        iota_row = AR2[0:1, 0:4096].bitcast(F32)[0:1, 0:T]
        e_iot = p.dma("s", iota_row, iota_tD.ap())
        if LV < 1:
            return [e_iot]

        def ms_f(e):
            e.memset(c_n1, 128.0)
            e.memset(c_n2, 2048.0)
            e.memset(c_eps5, 1e-5)
            e.memset(c_eps8, 1e-8)
            return e.memset(c_n3, 8192.0)
        e_ms = p.c("pool", ms_f, waits=cl)

        def sc_f(e):
            e.activation(c_n1, c_n1, AF.Ln)
            e.activation(c_n1, c_n1, AF.Exp, scale=-0.5)
            e.activation(c_n2, c_n2, AF.Ln)
            e.activation(c_n2, c_n2, AF.Exp, scale=-0.5)
            e.activation(c_n3, c_n3, AF.Ln)
            return e.activation(c_n3, c_n3, AF.Exp, scale=-0.5)
        e_sc = p.c("act", sc_f, waits=[e_ms])

        def bc_f(e):
            e.matmul(PQ[0][:, 0:1], c_onesrow32, c_n1, start=True, stop=True)
            e.matmul(PQ[0][:, 1:2], c_onesrow32, c_n2, start=True, stop=True)
            return e.matmul(PQ[0][:, 2:3], c_onesrow32, c_n3, start=True, stop=True)
        e_bc = p.c("pe", bc_f, waits=[e_sc])
        def rs_f(e):
            e.activation(c_rs128, PQ[0][:, 0:1], AF.Identity)
            e.activation(c_r2048c, PQ[0][:, 1:2], AF.Identity)
            return e.activation(c_r8192c, PQ[0][:, 2:3], AF.Identity)
        e_rs = p.c("act", rs_f, waits=[e_bc])

        def fold_f(e):
            e.tensor_tensor(c_sup, c_sup, c_r2048c.broadcast_to([128, 64]),
                            ALU.mult)
            e.tensor_tensor(c_sup, c_sup, c_su_dpv, ALU.mult)
            e.tensor_tensor(c_sgt, c_sgt, c_r2048c.broadcast_to([128, 64]),
                            ALU.mult)
            return e.tensor_tensor(c_sdn, c_sdn,
                                   c_r8192c.broadcast_to([128, 16]), ALU.mult)
        e_fold = p.c("dve", fold_f, waits=[e_rs])
        if LV < 2:
            return [e_fold]

        th = AR4[0:64, 0:2048]
        ktab = AR4[0:64, 2048:4096]

        def invf_f(e):
            e.tensor_scalar_mul(c_scr0, c_iota_p, C_HI)
            return e.tensor_scalar_mul(c_scr1, c_iota_p, C_LO)
        e_if = p.c("dve", invf_f, waits=cl)
        e_if2 = p.c("dve", lambda e: e.tensor_add(c_scrS, c_scr0, c_scr1),
                    waits=[e_if, e_fold])

        def ifx_f(e):
            e.activation(c_scrE, c_scrS, AF.Exp)
            return e.activation(c_identf[:, 0:64], c_identf[:, 0:64],
                                AF.Identity)
        e_ifx = p.c("act", ifx_f, waits=[e_if2])

        def sincos(dst_tab, kD, phase_shift, e_war):
            e_t = p.dma("s", th, iota2dD.ap(), waits=[e_war])
            e_k = p.dma("s", ktab, kD.ap(), waits=[e_war])
            if phase_shift:
                e0 = p.c("act", lambda e: e.activation(
                    th, th, AF.Identity, scale=c_invf64), waits=[e_t, e_ifx])
                e0 = p.c("dve", lambda e: e.tensor_scalar(
                    th, th, PIH_HI, PIH_LO, ALU.add, ALU.add), waits=[e0])
            else:
                e0 = p.c("act", lambda e: e.activation(
                    th, th, AF.Identity, scale=c_invf64), waits=[e_t, e_ifx])
            e_r1 = p.c("dve", lambda e: e.scalar_tensor_tensor(
                th, ktab, -TPI_HI, th, ALU.mult, ALU.add), waits=[e0, e_k])
            e_r2 = p.c("dve", lambda e: e.scalar_tensor_tensor(
                th, ktab, -TPI_LO, th, ALU.mult, ALU.add), waits=[e_r1])

            def clamp_f(e):
                e.tensor_scalar_min(th, th, PI_F32)
                return e.tensor_scalar_max(th, th, -PI_F32)
            e_cl = p.c("dve", clamp_f, waits=[e_r2])
            e_sin = p.c("act", lambda e: e.activation(th, th, AF.Sin),
                        waits=[e_cl])
            e_ev = None
            for tt in range(16):
                e_tr = p.c("pe", lambda e, tt=tt: e.transpose(
                    PQ[3][:, 0:64], th[0:64, tt * 128:(tt + 1) * 128],
                    c_identf[0:64, 0:64]), waits=[e_sin, e_ev])
                e_ev = p.c("act", lambda e, tt=tt: e.activation(
                    dst_tab[:, tt * 64:(tt + 1) * 64], PQ[3][:, 0:64],
                    AF.Identity), waits=[e_tr])
            return e_ev

        e_stab = sincos(c_stab, k1tabD, False, e_rs)
        e_ctab = sincos(c_ctab, k2tabD, True, e_stab)
        return [e_fold, e_ctab]

    ln_ps = {"pq5": None, "pq6": None}

    def ln_phase(src_dram, h_dst_fn, su_fold, tok_lo, tok_hi, h_war=()):
        xmini = AR4[:, 0:SEG * 16].rearrange("p (c t) -> p c t", c=16)
        xb = AR3[:, 0:SEG * 16].rearrange("p (c t) -> p c t", c=16)
        e_h = list(h_war)
        e_s2 = None
        e_bce = None
        for g in range((tok_hi - tok_lo) // SEG):
            lo = tok_lo + g * SEG
            ld = p.dma("s", xmini[:, :, :],
                       src_dram.ap().rearrange("(c p) t -> p c t", p=128)
                       [:, :, lo:lo + SEG],
                       waits=e_h[-2:])

            def cast_f(e):
                last = None
                for ch in range(16):
                    last = e.tensor_copy(xb[:, ch, :], xmini[:, ch, :])
                return last
            e_cast = p.c("dve", cast_f, waits=[ld, e_s2])

            def s1_f(e):
                last = None
                for ch in range(16):
                    last = e.matmul(PQ[4][0:1, 0:SEG], c_onescol, xb[:, ch, :],
                                    start=(ch == 0), stop=(ch == 15))
                return last
            e_s1 = p.c("pe", s1_f, waits=[e_cast, ln_ps["pq5"]])

            def sq_f(e):
                last = None
                for ch in range(16):
                    last = e.activation(xb[:, ch, :], xb[:, ch, :], AF.Square)
                return last
            e_sq = p.c("act", sq_f, waits=[e_s1])

            def s2_f(e):
                last = None
                for ch in range(16):
                    last = e.matmul(PQ[5][0:1, 0:SEG], c_onescol, xb[:, ch, :],
                                    start=(ch == 0), stop=(ch == 15))
                return last
            e_s2 = p.c("pe", s2_f, waits=[e_sq, ln_ps["pq6"]])

            def rows_f(e):
                e.activation(r_mrow, PQ[4][0:1, 0:SEG], AF.Identity,
                             scale=1.0 / 2048.0)
                return e.activation(r_e2row, PQ[5][0:1, 0:SEG], AF.Identity,
                                    scale=1.0 / 2048.0)
            e_rows = p.c("act", rows_f, waits=[e_s1, e_s2])

            def var_f(e):
                e.tensor_tensor(r_tmprow, r_mrow, r_mrow, ALU.mult)
                return e.tensor_tensor(r_e2row, r_e2row, r_tmprow, ALU.subtract)
            e_var = p.c("dve", var_f, waits=[e_rows])

            def rstd_f(e):
                e.activation(r_e2row, r_e2row, AF.Ln, bias=1e-5)
                return e.activation(r_e2row, r_e2row, AF.Exp, scale=-0.5)
            e_rstd = p.c("act", rstd_f, waits=[e_var])

            def bc_f(e):
                e.matmul(PQ[4][:, 0:SEG], c_onesrow32, r_mrow,
                         start=True, stop=True)
                return e.matmul(PQ[5][:, 0:SEG], c_onesrow32, r_e2row,
                                start=True, stop=True)
            e_bc = p.c("pe", bc_f, waits=[e_rstd])

            def bce_f(e):
                e.activation(r_mb, PQ[4][:, 0:SEG], AF.Identity)
                return e.activation(r_rb, PQ[5][:, 0:SEG], AF.Identity)
            e_bce = p.c("act", bce_f, waits=[e_bc] + e_h[-2:])
            ln_ps["pq5"] = e_bce
            ln_ps["pq6"] = e_bce

            def mk_h(chs, lo=None, use_stt=False):
                def h_f(e, lo=lo):
                    last = None
                    for ch in chs:
                        dst = h_dst_fn(ch, lo, SEG)
                        e.tensor_tensor(xmini[:, ch, :], xmini[:, ch, :], r_mb,
                                        ALU.subtract)
                        if su_fold and use_stt:
                            last = e.scalar_tensor_tensor(
                                dst, xmini[:, ch, :], c_su_ug[:, ch:ch + 1],
                                r_rb, ALU.mult, ALU.mult)
                        else:
                            last = e.tensor_tensor(dst, xmini[:, ch, :], r_rb,
                                                   ALU.mult)
                    return last
                return h_f
            e_h1 = p.c("dve", mk_h(range(0, 8), lo, True), waits=[e_bce])
            e_h2 = p.c("pool", mk_h(range(8, 16), lo), waits=[e_bce])
            if su_fold:
                def su_f(e, lo=lo):
                    last = None
                    for ch in range(8, 16):
                        dst = h_dst_fn(ch, lo, SEG)
                        last = e.tensor_scalar_mul(dst, dst,
                                                   c_su_ug[:, ch:ch + 1])
                    return last
                e_h2 = p.c("dve", su_f, waits=[e_h2])
            e_h.append(e_h1)
            e_h.append(e_h2)
        return e_h[-2:]

    def qk_proj(wD, dst_dram, h_evts, pst):
        e_cons = {-2: pst.get("cons"), -1: pst.get("cons")}
        e_quad = pst.setdefault("quad", {})
        e_stage = pst.setdefault("stage", {})
        last = None
        for sl in range(8):
            buf = AR3[:, (sl % 2) * 4096:(sl % 2) * 4096 + 4096]
            bufv = buf.rearrange("p (jc hc j) -> p jc hc j", jc=2, hc=16)
            ld = p.dma("g", bufv,
                       wD.ap()[2 * sl:2 * sl + 2].rearrange(
                           "jc hc hp j -> hp jc hc j"),
                       waits=[e_cons.get(sl - 2)])
            for jc2 in range(2):
                jch = 2 * sl + jc2
                for tq in range(4):
                    qi = (jc2 * 4 + tq) % 4
                    qd = PQ[qi]

                    def mm_f(e, bufv=bufv, jc2=jc2, tq=tq, qd=qd):
                        ll = None
                        for hc in range(16):
                            ll = e.matmul(
                                qd[:, :], bufv[:, jc2, hc, :],
                                h_ap(hc)[:, tq * 512:(tq + 1) * 512],
                                start=(hc == 0), stop=(hc == 15))
                        return ll
                    e_mm = p.c("pe", mm_f,
                               waits=[ld, e_quad.get(qi)] + list(h_evts))
                    e_cons[sl] = e_mm
                    stg = AR2[:, qi * 512:(qi + 1) * 512]
                    e_ev = p.c("act", lambda e, qd=qd, stg=stg: e.activation(
                        stg, qd[:, :], AF.Identity),
                        waits=[e_mm, e_stage.get(qi)])
                    e_quad[qi] = e_ev
                    e_st = p.dma("s",
                                 dst_dram.ap()[jch * 128:(jch + 1) * 128,
                                               tq * 512:(tq + 1) * 512],
                                 stg, waits=[e_ev])
                    e_stage[qi] = e_st
                    last = e_st
        pst["cons"] = e_cons[7]
        return last

    def v_proj(h_evts, pst):
        lds = []
        for i in range(4):
            lds.append(p.dma(
                "g",
                AR1[:, 32768 + i * 8192:32768 + (i + 1) * 8192].rearrange(
                    "p (c j) -> p c j", c=4),
                wvrD.ap()[4 * i:4 * i + 4].rearrange("c hp j -> hp c j")))
        e_quad = pst.setdefault("quad", {})
        e_stage = pst.setdefault("stage", {})
        last = None
        for tt in range(16):
            for js in range(4):
                qi = (tt * 4 + js) % 4
                qd = PQ[qi]

                def mm_f(e, tt=tt, js=js, qd=qd):
                    ll = None
                    for hc in range(16):
                        ll = e.matmul(
                            qd[:, :], h_ap(hc)[:, tt * 128:(tt + 1) * 128],
                            vb_ap(hc)[:, js * 512:(js + 1) * 512],
                            start=(hc == 0), stop=(hc == 15))
                    return ll
                e_mm = p.c("pe", mm_f,
                           waits=[lds[-1], e_quad.get(qi)] + list(h_evts))
                stg = AR2[:, qi * 512:(qi + 1) * 512]
                e_ev = p.c("act", lambda e, qd=qd, stg=stg: e.activation(
                    stg, qd[:, :], AF.Identity), waits=[e_mm, e_stage.get(qi)])
                e_quad[qi] = e_ev
                e_st = p.dma("s", vt_s.ap()[tt * 128:(tt + 1) * 128,
                                            js * 512:(js + 1) * 512],
                             stg, waits=[e_ev])
                e_stage[qi] = e_st
                last = e_st
        return last

    def attn_core(kind, qfD, kfD, vD, vcol0, eps):
        def qs_ap(h):
            return AR2[:, h * 512:(h + 1) * 512]

        def aout_ap(h):
            return AR2[:, 8192 + h * 512:8192 + (h + 1) * 512]

        p_t = [AR3[:, 0:512], AR3[:, 512:1024]]
        negm_bf = AR3[0:1, 1024:1536]
        sq_t = AR4[:, 0:512]
        rden_b = AR4[:, 512:1024]
        mcols = AR4[:, 1024:1028]
        mtmp = AR4[:, 1028:1029]

        e_aout_store = None
        e_psum = {}
        e_p_free = {}
        last = None
        for s in range(4):
            ev_k = p.dma(
                "s", AR1[:, 0:32768].rearrange("p (h c) -> p h c", h=16)
                [:, :, s * 512:(s + 1) * 512],
                kfD.ap().rearrange("(h p) t -> p h t", p=128)
                [:, :, s * 512:(s + 1) * 512])
            ev_v = p.dma(
                "s", AR1[:, 32768:65536].rearrange("p (c j) -> p c j", c=16)
                [:, 4 * s:4 * s + 4, :],
                vD.ap().rearrange("(c p) j -> p c j", p=128)
                [:, 4 * s:4 * s + 4, vcol0:vcol0 + 2048])
            ev_q = p.dma(
                "s", AR2[:, 0:8192].rearrange("p (h c) -> p h c", h=16),
                qfD.ap().rearrange("(h p) t -> p h t", p=128)
                [:, :, s * 512:(s + 1) * 512],
                waits=[e_aout_store])
            base_w = [ev_k, ev_v, ev_q]
            nkc = 4 * (s + 1)
            for h in range(16):
                e_negm = None
                if kind == "soft":
                    e_red = None
                    for tau in range(4):
                        L = s * 512 + (tau + 1) * 128
                        nkb = (L + 511) // 512
                        for kb in range(nkb):
                            n = min(512, L - kb * 512)
                            qd = PQ[kb % 2]
                            e_mm = p.c("pe", lambda e, h=h, tau=tau, kb=kb, n=n,
                                       qd=qd: e.matmul(
                                           qd[:, 0:n],
                                           qs_ap(h)[:, tau * 128:(tau + 1) * 128],
                                           k_ap(h)[:, kb * 512:kb * 512 + n],
                                           start=True, stop=True),
                                       waits=base_w + [e_psum.get(kb % 2)])

                            def red_f(e, qd=qd, n=n, kb=kb, tau=tau, nkb=nkb):
                                if kb == nkb - 1:
                                    e.tensor_tensor(qd[:, n - 128:n],
                                                    qd[:, n - 128:n],
                                                    c_maskA, ALU.add)
                                e.reduce_max(mtmp, qd[:, 0:n], axis=AX.X)
                                if kb == 0:
                                    return e.tensor_copy(mcols[:, tau:tau + 1],
                                                         mtmp)
                                return e.tensor_tensor(
                                    mcols[:, tau:tau + 1],
                                    mcols[:, tau:tau + 1], mtmp, ALU.max)
                            e_red = p.c("dve", red_f, waits=[e_mm, e_red])
                            e_psum[kb % 2] = e_red
                    e_tr = None
                    for tau in range(4):
                        e_tr = p.c("pe", lambda e, tau=tau: e.transpose(
                            PQ[4][0:1, tau * 128:(tau + 1) * 128],
                            mcols[:, tau:tau + 1], c_identf),
                            waits=[e_red, e_psum.get(4), e_tr])
                    e_negm = p.c("act", lambda e: e.activation(
                        negm_bf, PQ[4][0:1, 0:512], AF.Identity, scale=-1.0),
                        waits=[e_tr])
                    e_psum[4] = e_negm

                e_exp = None
                e_av = None
                e_den = None
                for ki, kc in enumerate(range(nkc - 1, -1, -1)):
                    first, lastk = (ki == 0), (kc == 0)
                    lo = max(0, kc - 4 * s) * 128
                    n = 512 - lo
                    si = 2 + (kc % 2)
                    qd = PQ[si]

                    def st_f(e, h=h, kc=kc, lo=lo, n=n, qd=qd, kind=kind):
                        mm = e.matmul(qd[:, lo:lo + n],
                                      k_ap(h)[:, kc * 128:(kc + 1) * 128],
                                      qs_ap(h)[:, lo:lo + n],
                                      start=True, stop=(kind != "soft"))
                        if kind == "soft":
                            mm = e.matmul(qd[:, lo:lo + n], c_onesrow,
                                          negm_bf[0:1, lo:lo + n],
                                          start=False, stop=True)
                        return mm
                    e_st = p.c("pe", st_f,
                               waits=base_w + [e_psum.get(si), e_negm])

                    pt = p_t[kc % 2]
                    if kind == "soft":
                        e_exp = p.c("act", lambda e, qd=qd, lo=lo, n=n, pt=pt:
                                    e.activation(pt[:, lo:lo + n],
                                                 qd[:, lo:lo + n], AF.Exp,
                                                 scale=c_rs128),
                                    waits=[e_st, e_p_free.get(kc % 2), e_exp])
                    else:
                        e_sq1 = p.c("act", lambda e, qd=qd, lo=lo, n=n:
                                    e.activation(sq_t[:, lo:lo + n],
                                                 qd[:, lo:lo + n], AF.Square),
                                    waits=[e_st, e_exp])
                        e_exp = p.c("act", lambda e, lo=lo, n=n, pt=pt:
                                    e.activation(pt[:, lo:lo + n],
                                                 sq_t[:, lo:lo + n], AF.Square),
                                    waits=[e_sq1, e_p_free.get(kc % 2)])
                    e_psum[si] = e_exp
                    if kc >= 4 * s:
                        e_exp = p.c("dve", lambda e, pt=pt, lo=lo:
                                    e.tensor_tensor(pt[:, lo:lo + 128],
                                                    pt[:, lo:lo + 128],
                                                    c_maskT, ALU.mult),
                                    waits=[e_exp])
                        if lo > 0:
                            e_exp = p.c("pool", lambda e, pt=pt, lo=lo:
                                        e.memset(pt[:, 0:lo], 0.0),
                                        waits=[e_exp, e_p_free.get(kc % 2)])

                    e_av = p.c("pe", lambda e, kc=kc, h=h, pt=pt,
                               first=first, lastk=lastk:
                               e.matmul(PQ[5][:, 0:512],
                                        vb_ap(kc)[:, h * 128:(h + 1) * 128],
                                        pt[:, 0:512],
                                        start=first, stop=lastk),
                               waits=[e_exp] + ([e_psum.get(5)] if first else []))
                    e_den = p.c("pe", lambda e, kc=kc, pt=pt,
                                first=first, lastk=lastk:
                                e.matmul(PQ[4][0:1, 0:512], c_onescol,
                                         pt[:, 0:512],
                                         start=first, stop=lastk),
                                waits=([e_psum.get(4)] if first else []))
                    e_p_free[kc % 2] = e_den

                def div_f(e, eps=eps):
                    if eps:
                        e.activation(a_denr, PQ[4][0:1, 0:512], AF.Ln, bias=eps)
                    else:
                        e.activation(a_denr, PQ[4][0:1, 0:512], AF.Ln)
                    return e.activation(a_denr, a_denr, AF.Exp, scale=-1.0)
                e_div = p.c("act", div_f, waits=[e_den])
                e_psum[4] = e_div
                e_rbc = p.c("pe", lambda e: e.matmul(
                    PQ[4][:, 0:512], c_onesrow32, a_denr, start=True, stop=True),
                    waits=[e_div, e_psum.get(4)])
                e_rbe = p.c("act", lambda e: e.activation(
                    rden_b, PQ[4][:, 0:512], AF.Identity), waits=[e_rbc])
                e_psum[4] = e_rbe
                e_out = p.c("dve", lambda e, h=h: e.tensor_tensor(
                    aout_ap(h), PQ[5][:, 0:512], rden_b, ALU.mult),
                    waits=[e_rbe, e_av, e_aout_store])
                e_psum[5] = e_out
                last = e_out
            e_aout_store = p.dma(
                "s", ao_s.ap().rearrange("(h p) t -> p h t", p=128)
                [:, :, s * 512:(s + 1) * 512],
                AR2[:, 8192:16384].rearrange("p (h c) -> p h c", h=16),
                waits=[last])
        return e_aout_store

    def out_proj(wD, xin_dram, xout_dram, e_ao_ready):
        lds = []
        for i in range(4):
            lds.append(p.dma(
                "s", AR1[:, i * 8192:(i + 1) * 8192].rearrange(
                    "p (c t) -> p c t", c=4),
                ao_s.ap().rearrange("(c p) t -> p c t", p=128)
                [:, 4 * i:4 * i + 4, :],
                waits=[e_ao_ready]))
        e_cons = {}
        e_quad = {}
        e_xin = {}
        e_xout = {}
        last = None
        for sl in range(8):
            buf = AR3[:, (sl % 2) * 4096:(sl % 2) * 4096 + 4096]
            bufv = buf.rearrange("p (oc jc o) -> p oc jc o", oc=2, jc=16)
            ld = p.dma("g", bufv,
                       wD.ap()[2 * sl:2 * sl + 2].rearrange(
                           "oc jc jp o -> jp oc jc o"),
                       waits=[e_cons.get(sl - 2)])
            for oc2 in range(2):
                oc = 2 * sl + oc2
                for tq in range(4):
                    qi = (oc2 * 4 + tq) % 4
                    qd = PQ[qi]
                    xi = (oc2 * 4 + tq) % 2
                    xin_t = AR4[:, xi * 512:(xi + 1) * 512]
                    xout_t = AR4[:, 1024 + xi * 512:1024 + (xi + 1) * 512]
                    e_xl = p.dma("s", xin_t,
                                 xin_dram.ap()[oc * 128:(oc + 1) * 128,
                                               tq * 512:(tq + 1) * 512],
                                 waits=[e_xin.get(xi)])

                    def mm_f(e, bufv=bufv, oc2=oc2, tq=tq, qd=qd):
                        ll = None
                        for jc in range(16):
                            ll = e.matmul(qd[:, :], bufv[:, oc2, jc, :],
                                          ao_ap(jc)[:, tq * 512:(tq + 1) * 512],
                                          start=(jc == 0), stop=(jc == 15))
                        return ll
                    e_mm = p.c("pe", mm_f, waits=[ld, lds[-1], e_quad.get(qi)])
                    e_cons[sl] = e_mm
                    e_add = p.c("dve", lambda e, qd=qd, xin_t=xin_t,
                                xout_t=xout_t: e.tensor_tensor(
                                    xout_t, qd[:, :], xin_t, ALU.add),
                                waits=[e_mm, e_xl, e_xout.get(xi)])
                    e_quad[qi] = e_add
                    e_xin[xi] = e_add
                    e_st = p.dma("s",
                                 xout_dram.ap()[oc * 128:(oc + 1) * 128,
                                                tq * 512:(tq + 1) * 512],
                                 xout_t, waits=[e_add])
                    e_xout[xi] = e_st
                    last = e_st
        return last

    def ffn(xin_dram, xout_dram):
        last = None
        for half in range(2):
            t0 = half * 1024
            htmp = AR3[:, 4096:8192].rearrange("p (c t) -> p c t", c=16)
            e_sa_all = []
            e_sa_last = []
            for g in range(1024 // SEG):
                lo = t0 + g * SEG
                evs = ln_phase(xin_dram, lambda ch, l, s_: htmp[:, ch, :],
                               True, lo, lo + SEG, h_war=e_sa_last)
                e_sa_last = []
                for chp in range(8):
                    qd = PQ[chp % 2]

                    def samm_f(e, chp=chp, qd=qd):
                        e.matmul(qd[:, 0:SEG], c_h128, htmp[:, 2 * chp, :],
                                 start=True, stop=True)
                        return e.matmul(qd[:, SEG:2 * SEG], c_h128,
                                        htmp[:, 2 * chp + 1, :],
                                        start=True, stop=True)
                    e_mm = p.c("pe", samm_f,
                               waits=evs + [e_sa_all[-2] if len(e_sa_all) >= 2
                                            else None])

                    def saev_f(e, chp=chp, qd=qd, g=g):
                        e.activation(u_ap(2 * chp)[:, g * SEG:(g + 1) * SEG],
                                     qd[:, 0:SEG], AF.Identity)
                        return e.activation(
                            u_ap(2 * chp + 1)[:, g * SEG:(g + 1) * SEG],
                            qd[:, SEG:2 * SEG], AF.Identity)
                    e_ev = p.c("act", saev_f, waits=[e_mm])
                    e_sa_all.append(e_ev)
                    e_sa_last.append(e_mm)
                if dbg1 is not None and half == 1 and g == 3:
                    p.dma("s", dbg1.ap(), AR3[:, 4096:8192], waits=e_sa_last[-2:])
            e_tap = None
            if dbg2 is not None and half == 1:
                e_tap = p.dma("s", dbg2.ap(), AR2[:, 0:16384], waits=e_sa_all[-2:])
            tmp_t = AR3[:, 0:1024]
            tmp_t2 = AR3[:, 1024:2048]
            e_bf = e_sa_all[-2:] + ([e_tap] if dbg2 is not None else [])
            for span in (1, 2, 4, 8):
                nxt = []
                pairs = [b for b in range(16) if not (b & span)]
                for i, b in enumerate(pairs):
                    eng = "dve" if i % 2 == 0 else "pool"
                    tt_ = tmp_t if i % 2 == 0 else tmp_t2

                    def bf_f(e, a=b, c=b + span, tt_=tt_):
                        e.tensor_tensor(tt_, u_ap(a), u_ap(c), ALU.subtract)
                        e.tensor_tensor(u_ap(a), u_ap(a), u_ap(c), ALU.add)
                        return e.tensor_copy(u_ap(c), tt_)
                    nxt.append(p.c(eng, bf_f, waits=e_bf))
                e_bf = nxt[-2:]
            e_ubf = e_bf

            e_w = {}
            e_ze = None
            for ic in range(64):
                ub = AR3[:, 4096 + (ic % 2) * 2048:4096 + (ic % 2) * 2048 + 2048]
                gb = AR3[:, (ic % 2) * 2048:(ic % 2) * 2048 + 2048]
                ubv = ub.rearrange("p (hc j) -> p hc j", hc=16)
                gbv = gb.rearrange("p (hc j) -> p hc j", hc=16)
                lu = p.dma("g", ubv,
                           wupTD.ap()[ic].rearrange("hc hp j -> hp hc j"),
                           waits=[e_w.get(("ucons", ic - 2))]
                           + (e_ubf if ic < 2 else []))
                lg = p.dma("g", gbv,
                           wgTD.ap()[ic].rearrange("hc hp j -> hp hc j"),
                           waits=[e_w.get(("gcons", ic - 2))]
                           + (e_ubf if ic < 2 else []))
                for tqh in range(2):
                    qu = PQ[tqh]
                    qg = PQ[2 + tqh]

                    def up_f(e, ubv=ubv, tqh=tqh, qd=qu):
                        ll = None
                        for hc in range(16):
                            ll = e.matmul(qd[:, :], ubv[:, hc, :],
                                          u_ap(hc)[:, tqh * 512:(tqh + 1) * 512],
                                          start=(hc == 0), stop=(hc == 15))
                        return ll
                    e_up = p.c("pe", up_f,
                               waits=[lu, e_w.get(("psu", tqh))] + e_ubf)
                    e_w[("ucons", ic)] = e_up

                    def gt_f(e, gbv=gbv, tqh=tqh, qd=qg):
                        ll = None
                        for hc in range(16):
                            ll = e.matmul(qd[:, :], gbv[:, hc, :],
                                          u_ap(hc)[:, tqh * 512:(tqh + 1) * 512],
                                          start=(hc == 0), stop=(hc == 15))
                        return ll
                    e_gt = p.c("pe", gt_f,
                               waits=[lg, e_w.get(("psg", tqh))] + e_ubf)
                    e_w[("gcons", ic)] = e_gt

                    sg = AR4[:, tqh * 512:(tqh + 1) * 512]
                    e_si = p.c("act", lambda e, qd=qg, sg=sg, ic=ic:
                               e.activation(sg, qd[:, :], AF.Silu,
                                            scale=c_sgt[:, ic:ic + 1]),
                               waits=[e_gt, e_w.get(("sg", tqh))])
                    e_w[("psg", tqh)] = e_si
                    zt = AR4[:, 1024 + tqh * 256:1024 + (tqh + 1) * 256].bitcast(BF16)
                    e_z = p.c("dve", lambda e, qd=qu, sg=sg, zt=zt, ic=ic:
                              e.scalar_tensor_tensor(zt, qd[:, :],
                                                     c_sup[:, ic:ic + 1], sg,
                                                     ALU.mult, ALU.mult),
                              waits=[e_si, e_up, e_w.get(("zt", tqh))])
                    e_w[("psu", tqh)] = e_z
                    e_w[("sg", tqh)] = e_z
                    e_sa = p.c("pe", lambda e, zt=zt: e.matmul(
                        PQ[4][:, :], c_h128, zt, start=True, stop=True),
                        waits=[e_z, e_ze])
                    e_ze = p.c("act", lambda e, ic=ic, tqh=tqh: e.activation(
                        za_ap(ic)[:, tqh * 512:(tqh + 1) * 512], PQ[4][:, :],
                        AF.Identity), waits=[e_sa])
                    e_w[("zt", tqh)] = e_sa
                    e_w[("lastz", tqh)] = e_z
            e_bf = [e_ze]
            for span in (1, 2, 4, 8, 16, 32):
                nxt = []
                pairs = [b for b in range(64) if not (b & span)]
                for i, b in enumerate(pairs):
                    eng = "dve" if i % 2 == 0 else "pool"
                    tt_ = tmp_t if i % 2 == 0 else tmp_t2

                    def bf_f(e, a=b, c=b + span, tt_=tt_):
                        e.tensor_tensor(tt_, za_ap(a), za_ap(c), ALU.subtract)
                        e.tensor_tensor(za_ap(a), za_ap(a), za_ap(c), ALU.add)
                        return e.tensor_copy(za_ap(c), tt_)
                    nxt.append(p.c(eng, bf_f, waits=e_bf))
                e_bf = nxt[-2:]

            e_parts = {}
            e_ps = {}
            e_x = {}
            for oc in range(16):
                bvs = []
                for hh in range(2):
                    b = AR3[:, 4096 * hh:4096 * hh + 4096]
                    bv = b.rearrange("p (ic o) -> p ic o", ic=32)
                    ld = p.dma("g", bv,
                               wdnTD.ap()[oc, 32 * hh:32 * hh + 32].rearrange(
                                   "ic ip o -> ip ic o"),
                               waits=[e_parts.get((oc - 1, hh, 0)),
                                      e_parts.get((oc - 1, hh, 1))]
                               + (e_bf if oc < 2 else []))
                    bvs.append((bv, ld))
                for tqh in range(2):
                    qd = PQ[tqh]
                    xin_t = AR4[:, tqh * 512:(tqh + 1) * 512]
                    xout_t = AR4[:, 1024 + tqh * 512:1024 + (tqh + 1) * 512]
                    e_xl = p.dma("s", xin_t,
                                 xin_dram.ap()[oc * 128:(oc + 1) * 128,
                                               t0 + tqh * 512:t0 + (tqh + 1) * 512],
                                 waits=[e_x.get(("in", tqh)),
                                        e_w.get(("lastz", 0)) if oc == 0 else None,
                                        e_w.get(("lastz", 1)) if oc == 0 else None])

                    def dnA_f(e, bv=bvs[0][0], tqh=tqh, qd=qd):
                        ll = None
                        for ic in range(32):
                            ll = e.matmul(qd[:, :], bv[:, ic, :],
                                          za_ap(ic)[:, tqh * 512:(tqh + 1) * 512],
                                          start=(ic == 0), stop=False)
                        return ll
                    e_A = p.c("pe", dnA_f,
                              waits=[bvs[0][1], e_ps.get(tqh)] + e_bf)
                    e_parts[(oc, 0, tqh)] = e_A

                    def dnB_f(e, bv=bvs[1][0], tqh=tqh, qd=qd):
                        ll = None
                        for ic in range(32):
                            ll = e.matmul(qd[:, :], bv[:, ic, :],
                                          za_ap(32 + ic)[:, tqh * 512:(tqh + 1) * 512],
                                          start=False, stop=(ic == 31))
                        return ll
                    e_B = p.c("pe", dnB_f, waits=[bvs[1][1]])
                    e_parts[(oc, 1, tqh)] = e_B
                    e_ev = p.c("dve", lambda e, qd=qd, oc=oc, xin_t=xin_t,
                               xout_t=xout_t: e.scalar_tensor_tensor(
                                   xout_t, qd[:, :], c_sdn[:, oc:oc + 1],
                                   xin_t, ALU.mult, ALU.add),
                               waits=[e_B, e_xl, e_x.get(("out", tqh))])
                    e_ps[tqh] = e_ev
                    e_x[("in", tqh)] = e_ev
                    e_st = p.dma("s",
                                 xout_dram.ap()[oc * 128:(oc + 1) * 128,
                                                t0 + tqh * 512:t0 + (tqh + 1) * 512],
                                 xout_t, waits=[e_ev])
                    e_x[("out", tqh)] = e_st
                    last = e_st
            p.barrier()
        return last

    def qkv_proj_b(h_evts):
        e_cons = {}
        e_quad = {}
        e_stg = {}
        last = None
        for js in range(24):
            buf = AR3[:, (js % 2) * 4096:(js % 2) * 4096 + 4096]
            bufv = buf.rearrange("p (hc j) -> p hc j", hc=16)
            ld = p.dma("g", bufv,
                       wqkvD.ap().rearrange("hc hp j -> hp hc j")
                       [:, :, js * 256:(js + 1) * 256],
                       waits=[e_cons.get(js - 2)])
            for tg in range(4):
                stg = AR2[:, (tg % 2) * 1024:(tg % 2) * 1024 + 1024]
                e_ev = None
                for ti in range(4):
                    tt = tg * 4 + ti
                    qi = tt % 4
                    qd = PQ[qi]

                    def mm_f(e, bufv=bufv, tt=tt, qd=qd):
                        ll = None
                        for hc in range(16):
                            ll = e.matmul(qd[:, 0:256],
                                          h_ap(hc)[:, tt * 128:(tt + 1) * 128],
                                          bufv[:, hc, :],
                                          start=(hc == 0), stop=(hc == 15))
                        return ll
                    e_mm = p.c("pe", mm_f,
                               waits=[ld, e_quad.get(qi)] + list(h_evts))
                    e_cons[js] = e_mm
                    e_ev = p.c("act", lambda e, qd=qd, stg=stg, ti=ti:
                               e.activation(stg[:, ti * 256:(ti + 1) * 256],
                                            qd[:, 0:256], AF.Identity),
                               waits=[e_mm, e_stg.get(tg % 2)])
                    e_quad[qi] = e_ev
                e_st = p.dma("s",
                             qkv_s.ap().rearrange("(q p) j -> p q j", p=128)
                             [:, 4 * tg:4 * tg + 4, js * 256:(js + 1) * 256],
                             stg.rearrange("p (q j) -> p q j", q=4),
                             waits=[e_ev])
                e_stg[tg % 2] = e_st
                last = e_st
        return last

    def rope_ln_b():
        B2LV = int(os.environ.get("MK_B2", "9"))
        qn = AR4[:, 0:2048]
        qn3 = qn.rearrange("p (h two j) -> p h two j", h=16, two=2)
        qnh = qn.rearrange("p (h j) -> p h j", h=16)
        tmp = AR4[:, 2048:3072]
        tm3 = tmp.rearrange("p (h j) -> p h j", h=16)
        tmh = tmp.rearrange("p (h j) -> p h j", h=8)
        s1 = AR4[:, 3072:3088]
        s2 = AR4[:, 3088:3104]
        msq = AR4[:, 3104:3120]
        last = None
        e_app = None
        e_qb_free = {}
        e_slot = {}
        for tt in range(16):
            for which, colbase, dstD in (("q", 0, qf_s), ("k", 2048, kf_s)):
                tile = AR2[:, 0:2048] if which == "q" else AR2[:, 2048:4096]
                ld = p.dma("s", tile,
                           qkv_s.ap()[tt * 128:(tt + 1) * 128,
                                      colbase:colbase + 2048],
                           waits=[e_app])
                t3 = tile.rearrange("p (h two j) -> p h two j", h=16, two=2)
                q1v, q2v = t3[:, :, 0, :], t3[:, :, 1, :]
                qn1v, qn2v = qn3[:, :, 0, :], qn3[:, :, 1, :]
                cb = c_ctab[:, tt * 64:(tt + 1) * 64].rearrange(
                    "p (o j) -> p o j", o=1).broadcast_to([128, 16, 64])
                sbt = c_stab[:, tt * 64:(tt + 1) * 64].rearrange(
                    "p (o j) -> p o j", o=1).broadcast_to([128, 16, 64])

                def rope_f(e, q1v=q1v, q2v=q2v, qn1v=qn1v, qn2v=qn2v,
                           cb=cb, sbt=sbt):
                    tv = tm3[:, :, 0:64]
                    e.tensor_tensor(qn1v, q1v, cb, ALU.mult)
                    e.tensor_tensor(tv, q2v, sbt, ALU.mult)
                    e.tensor_tensor(qn1v, qn1v, tv, ALU.subtract)
                    e.tensor_tensor(qn2v, q1v, sbt, ALU.mult)
                    e.tensor_tensor(tv, q2v, cb, ALU.mult)
                    return e.tensor_tensor(qn2v, qn2v, tv, ALU.add)
                if B2LV < 2:
                    e_app = ld
                    continue
                e_rope = p.c("dve", rope_f, waits=[ld, e_app])
                if B2LV < 3:
                    e_app = e_rope
                    continue

                e_st1 = p.c("dve", lambda e: e.reduce_sum(s1, qnh, axis=AX.X),
                            waits=[e_rope])
                e_sqa = p.c("act", lambda e: e.activation(
                    tmp, qn[:, 0:1024], AF.Square), waits=[e_rope])
                e_r1 = p.c("dve", lambda e: e.reduce_sum(
                    s2[:, 0:8], tmh, axis=AX.X), waits=[e_sqa])
                e_sqb = p.c("act", lambda e: e.activation(
                    tmp, qn[:, 1024:2048], AF.Square), waits=[e_r1, e_sqa])
                e_r2 = p.c("dve", lambda e: e.reduce_sum(
                    s2[:, 8:16], tmh, axis=AX.X), waits=[e_sqb])

                def stat_f(e):
                    e.tensor_scalar_mul(s1, s1, 1.0 / 128.0)
                    e.tensor_scalar_add(tmp[:, 0:512], tmp[:, 0:512], 0.0)
                    e.tensor_scalar_mul(s2, s2, 1.0 / 128.0)
                    e.tensor_tensor(msq, s1, s1, ALU.mult)
                    e.tensor_scalar_add(tmp[:, 512:1024], tmp[:, 512:1024],
                                        0.0)
                    return e.tensor_tensor(s2, s2, msq, ALU.subtract)
                e_stat = p.c("dve", stat_f, waits=[e_r1, e_r2, e_st1])

                def rstd_f(e):
                    e.activation(s2, s2, AF.Ln, bias=1e-5)
                    return e.activation(s2, s2, AF.Exp, scale=-0.5)
                e_rstd = p.c("act", rstd_f, waits=[e_stat])
                if B2LV < 4:
                    e_app = e_rstd
                    continue

                qb = AR2[:, 4096:6144] if which == "q" else AR2[:, 6144:8192]
                s1b = s1.rearrange("p (h o) -> p h o", o=1).broadcast_to(
                    [128, 16, 128])
                s2b = s2.rearrange("p (h o) -> p h o", o=1).broadcast_to(
                    [128, 16, 128])
                qb3 = qb.rearrange("p (h j) -> p h j", h=16)

                def app_f(e, qb3=qb3, s1b=s1b, s2b=s2b):
                    e.tensor_tensor(qnh, qnh, s1b, ALU.subtract)
                    return e.tensor_tensor(qb3, qnh, s2b, ALU.mult)
                e_app = p.c("dve", app_f,
                            waits=[e_rstd, e_qb_free.get(which)])
                if B2LV < 5:
                    continue

                stg = AR2[:, 8192:10240] if which == "q" else AR2[:, 10240:12288]
                e_ev = None
                for h in range(16):
                    bank = [PMB, PMB2][h % 2]
                    e_tr = p.c("pe", lambda e, h=h, bank=bank, qb=qb:
                               e.transpose(bank[:, 0:128],
                                           qb[:, h * 128:(h + 1) * 128],
                                           c_ident),
                               waits=[e_app, e_slot.get(h % 2)])
                    e_ev = p.c("act", lambda e, h=h, bank=bank, stg=stg:
                               e.activation(stg[:, h * 128:(h + 1) * 128],
                                            bank[:, 0:128],
                                            AF.Identity, scale=c_lnsc,
                                            bias=c_lnbs),
                               waits=[e_tr, e_qb_free.get(("stg", which))])
                    e_slot[h % 2] = e_ev
                e_qb_free[which] = e_ev
                e_st = p.dma("s",
                             dstD.ap().rearrange("(h p) t -> p h t", p=128)
                             [:, :, tt * 128:(tt + 1) * 128],
                             stg.rearrange("p (h t) -> p h t", h=16),
                             waits=[e_ev])
                e_qb_free[("stg", which)] = e_st
                last = e_st
        return last

    _PHASE = os.environ.get("MK_PHASE", "")
    prologue()
    p.barrier()

    if _PHASE != "pro":
        h_evts = ln_phase(x0, lambda ch, lo, s_: h_ap(ch)[:, lo:lo + s_],
                          False, 0, T)
        if _PHASE != "ln":
            pst = {}
            qk_proj(wqTD, qf_s, h_evts, pst)
            qk_proj(wkTD, kf_s, h_evts, pst)
            v_proj(h_evts, pst)
            p.barrier()
            if _PHASE != "proj":
                attn_core("soft", qf_s, kf_s, vt_s, 0, 1e-8)
                p.barrier()
                if _PHASE != "attn":
                    out_proj(woTD, x0, xs1, None)
                    p.barrier()
    if _PHASE:
        p.barrier()
        p.dma("s", y.ap(), x0.ap())
        p.final_wait()
        p.emit()
        ctx.close()
        return nc

    if stage >= 2:
        ffn(xs1, xs2)
        p.barrier()
    if stage >= 3:
        _BP = os.environ.get("MK_BP", "")
        h_evts = ln_phase(xs2, lambda ch, lo, s_: h_ap(ch)[:, lo:lo + s_],
                          False, 0, T)
        qkv_proj_b(h_evts)
        p.barrier()
        if _BP != "b1":
            rope_ln_b()
            p.barrier()
            if _BP != "b2":
                attn_core("poly", qf_s, kf_s, qkv_s, 4096, 0.0)
                p.barrier()
                if _BP != "b3":
                    out_proj(woutTD, xs2, xs3, None)
                    p.barrier()
        if _BP:
            p.barrier()
            dmp = os.environ.get("MK_DUMP", "")
            if dmp == "tabs":
                p.dma("g", y.ap()[0:128, 0:2048], CON[:, 640:2688])
            elif dmp == "conf":
                p.dma("s", y.ap()[0:128, 0:640], CONF[:, 0:640])
            elif dmp == "qkv":
                p.dma("g", y.ap(), qkv_s.ap()[:, 0:2048])
            elif dmp in ("qf", "kf"):
                p.dma("g", y.ap(), {"qf": qf_s, "kf": kf_s}[dmp].ap())
            else:
                p.dma("s", y.ap(), xs2.ap())
            p.final_wait()
            p.emit()
            ctx.close()
            return nc
    dmp = os.environ.get("MK_DUMP", "")
    if dmp:
        p.barrier()
        if dmp == "qkv":
            p.dma("g", y.ap(), qkv_s.ap()[:, 0:2048])
        elif dmp == "tabs":
            p.dma("g", y.ap()[0:128, 0:2048], CON[:, 640:2688])
        elif dmp == "rope":
            p.dma("s", y.ap()[0:64, 0:2048], AR4[0:64, 0:2048])
            p.dma("s", y.ap()[64:128, 0:2048],
                  AR1[0:64, 24576:32768].bitcast(F32)[0:64, 0:2048])
            p.dma("s", y.ap()[128:192, 0:2048],
                  AR1[0:64, 16384:24576].bitcast(F32)[0:64, 0:2048])
        else:
            t = {"qf": qf_s, "kf": kf_s, "ao": ao_s, "x1": xs1, "x2": xs2,
                 "x3": xs3}[dmp]
            q = "g" if t.dtype == BF16 else "s"
            p.dma(q, y.ap(), t.ap())
    elif stage >= 4:
        ffn(xs3, y)
    else:
        src = {1: xs1, 2: xs2, 3: xs3}[stage]
        p.dma("s", y.ap(), src.ap())
    p.final_wait()
    p.emit()
    ctx.close()
    return nc


def _tile_lhsT(Wt):
    K, M = Wt.shape
    kc, mc = K // 128, M // 128
    return np.ascontiguousarray(
        Wt.reshape(kc, 128, mc, 128).transpose(2, 0, 1, 3))


def _ktab(shift):
    t = np.arange(2048, dtype=np.float64)
    invf = 1.0 / 10000.0 ** (np.arange(0, 128, 2, dtype=np.float64) / 128.0)
    th = invf[:, None] * t[None, :]
    if shift:
        th = th + math.pi / 2
    return np.round(th / (2 * math.pi)).astype(np.float32)


def _prep_inputs(i):
    f32 = np.float32
    H = sylvester(128).astype(f32)
    maskA = np.where(np.arange(128)[None, :] <= np.arange(128)[:, None],
                     0.0, MASK_NEG).astype(f32)
    maskT = np.where(np.arange(128)[None, :] >= np.arange(128)[:, None],
                     1.0, 0.0).astype(ml_dtypes.bfloat16)
    wsu_all = np.asarray(i["L0_ffn_Wscale_ug"], dtype=f32)
    shared = {
        "wqT": _tile_lhsT(np.ascontiguousarray(np.asarray(i["L0_attn_a_wq"], f32).T)),
        "wkT": _tile_lhsT(np.ascontiguousarray(np.asarray(i["L0_attn_a_wk"], f32).T)),
        "wvr": np.ascontiguousarray(np.asarray(i["L0_attn_a_wv"], f32).T).reshape(16, 128, D),
        "woT": _tile_lhsT(np.ascontiguousarray(np.asarray(i["L0_attn_a_wo"], f32).T)),
        "wqkv": np.ascontiguousarray(np.asarray(i["L0_attn_b_w_qkv"], f32)).reshape(16, 128, 3 * D),
        "woutT": _tile_lhsT(np.ascontiguousarray(np.asarray(i["L0_attn_b_w_out"], f32))),
        "wupT": _tile_lhsT(np.ascontiguousarray(np.asarray(i["L0_ffn_W_up"], f32).T)),
        "wgT": _tile_lhsT(np.ascontiguousarray(np.asarray(i["L0_ffn_W_gate"], f32).T)),
        "wdnT": _tile_lhsT(np.ascontiguousarray(np.asarray(i["L0_ffn_W_down"], f32).T)),
        "su_ug": np.ascontiguousarray(np.asarray(i["L0_ffn_SU_ug"], f32).reshape(16, 128).T),
        "su_dp": np.ascontiguousarray(np.asarray(i["L0_ffn_SU_dp"], f32).reshape(64, 128).T),
        "wsu": np.ascontiguousarray(wsu_all[:IN].reshape(64, 128).T),
        "wsg": np.ascontiguousarray(wsu_all[IN:].reshape(64, 128).T),
        "wsd": np.ascontiguousarray(np.asarray(i["L0_ffn_Wscale_dp"], f32).reshape(16, 128).T),
        "lnsc": np.ascontiguousarray(np.asarray(i["L0_attn_b_ln_scale"], f32).reshape(128, 1)),
        "lnbs": np.ascontiguousarray(np.asarray(i["L0_attn_b_ln_bias"], f32).reshape(128, 1)),
        "h128": H.astype(ml_dtypes.bfloat16),
        "identb": np.eye(128, dtype=ml_dtypes.bfloat16),
        "identf": np.eye(128, dtype=f32),
        "maskA": maskA,
        "maskT": maskT,
        "onescol": np.ones((128, 1), dtype=ml_dtypes.bfloat16),
        "onesrow": np.ones((1, 128), dtype=ml_dtypes.bfloat16),
        "onesrow32": np.ones((1, 128), dtype=f32),
        "iota_t": np.arange(T, dtype=f32).reshape(1, T),
        "iota_j": np.arange(128, dtype=f32).reshape(128, 1),
        "iota2d": np.tile(np.arange(T, dtype=f32)[None, :], (64, 1)),
        "k1tab": _ktab(False),
        "k2tab": _ktab(True),
    }
    shared = {k: np.ascontiguousarray(v) for k, v in shared.items()}
    x = np.asarray(i["L0_attn_a_x"], dtype=np.float32)
    in_maps = []
    for c in range(8):
        m = dict(shared)
        m["x0"] = np.ascontiguousarray(x[c].T)
        in_maps.append(m)
    return in_maps


def kernel(**inputs):
    in_maps = _prep_inputs(inputs)
    nc = build_program()
    res = run_bass_kernel_spmd(nc, in_maps, core_ids=list(range(8)))
    out = np.empty((8, T, D), dtype=np.float32)
    for c in range(8):
        out[c] = np.asarray(res.results[c]["y"], dtype=np.float32).T
    return out


# version 06
# speedup vs baseline: 1.1070x; 1.1070x over previous
import os
import math
import numpy as np
import ml_dtypes
import concourse.bass as bass
import concourse.mybir as mybir
from concourse.bass_utils import run_bass_kernel_spmd

F32 = mybir.dt.float32
BF16 = mybir.dt.bfloat16
I32 = mybir.dt.int32
AF = mybir.ActivationFunctionType
ALU = mybir.AluOpType
AX = mybir.AxisListType

D, T, IN = 2048, 2048, 8192
SEG = 256
_STAGE = int(os.environ.get("MK_STAGE", "4"))
_PARANOID = bool(int(os.environ.get("MK_PARANOID", "0")))


def _split_const(v, hi_bits):
    h = np.float32(v)
    u = np.uint32(int(h.view(np.uint32)) & ~((1 << (23 - hi_bits)) - 1))
    hi = u.view(np.float32)
    lo = np.float32(np.float64(v) - np.float64(hi))
    return float(hi), float(lo)


C_HI, C_LO = _split_const(-math.log(10000.0) / 64.0, 11)
TPI_HI, TPI_LO = _split_const(2.0 * math.pi, 10)
PIH_HI, PIH_LO = _split_const(math.pi / 2.0, 10)
INV_2PI = float(np.float32(1.0 / (2.0 * math.pi)))
PI_F32 = float(np.float32(math.pi))
MASK_NEG = -30000.0


class Prog:
    ENGINES = ("pe", "act", "dve", "pool", "sp")

    def __init__(self, nc, ctx):
        self.nc = nc
        self.stream = {e: [] for e in self.ENGINES}
        self.cnt = {}
        self.semh = {}
        for s in ("ldp", "lds", "pe", "act", "dve", "pool"):
            self.semh[s] = ctx.enter_context(nc.semaphore(s))
            self.cnt[s] = 0
        self._barrier = None
        self._bdone = set()

    def _add(self, eng, fn, waits, inc, amt):
        w = [x for x in waits if x is not None]
        if self._barrier is not None and eng not in self._bdone:
            w = list(self._barrier.items()) + w
            self._bdone.add(eng)
        if _PARANOID and eng in ("pe", "act", "dve", "pool") and inc == eng:
            w = [(eng, self.cnt[eng])] + w
        evt = None
        if inc is not None:
            self.cnt[inc] += amt
            evt = (inc, self.cnt[inc])
        self.stream[eng].append((fn, w, (inc, amt) if inc else None))
        return evt

    def c(self, eng, fn, waits=(), inc=True):
        return self._add(eng, fn, waits, eng if inc else None, 1)

    def dma(self, queue, dst, src, waits=(), **kw):
        eng = "pool" if queue == "g" else "sp"
        sem = "ldp" if queue == "g" else "lds"

        def fn(e, dst=dst, src=src, kw=kw):
            return e.dma_start(dst, src, **kw)
        return self._add(eng, fn, waits, sem, 16)

    def barrier(self):
        self._barrier = dict(self.cnt)
        self._bdone = set()

    def final_wait(self):
        waits = list(self.cnt.items())
        self._add("sp", lambda e: e.nop(), waits, None, 0)

    def emit(self):
        nc = self.nc
        eng_of = {"pe": "tensor", "act": "scalar", "dve": "vector",
                  "pool": "gpsimd", "sp": "sync"}
        with nc.Block() as blk:
            for ename in self.ENGINES:
                ops = self.stream[ename]
                if not ops:
                    continue

                def runner(e, ops=ops, self=self):
                    waited = {}
                    for fn, waits, inc in ops:
                        need = {}
                        for sem, val in waits:
                            if val > need.get(sem, 0):
                                need[sem] = val
                        for sem, val in need.items():
                            if waited.get(sem, 0) >= val:
                                continue
                            e.wait_ge(self.semh[sem], val)
                            waited[sem] = val
                        ins = fn(e)
                        if inc is not None:
                            sem, amt = inc
                            ins.then_inc(self.semh[sem], amt)
                getattr(blk, eng_of[ename])(runner)


def sylvester(n):
    H = np.array([[1.0]], dtype=np.float32)
    while H.shape[0] < n:
        H = np.block([[H, H], [H, -H]])
    return H


def build_program(stage=_STAGE):
    from contextlib import ExitStack
    ctx = ExitStack()
    nc = bass.Bass("TRN2", target_bir_lowering=False,
                   detect_race_conditions=_PARANOID)
    dt = nc.dram_tensor

    x0 = dt("x0", [D, T], F32, kind="ExternalInput")
    y = dt("y", [D, T], F32, kind="ExternalOutput")
    wqTD = dt("wqT", [16, 16, 128, 128], F32, kind="ExternalInput")
    wkTD = dt("wkT", [16, 16, 128, 128], F32, kind="ExternalInput")
    wvrD = dt("wvr", [16, 128, D], F32, kind="ExternalInput")
    woTD = dt("woT", [16, 16, 128, 128], F32, kind="ExternalInput")
    wqkvD = dt("wqkv", [16, 128, 3 * D], F32, kind="ExternalInput")
    woutTD = dt("woutT", [16, 16, 128, 128], F32, kind="ExternalInput")
    wupTD = dt("wupT", [64, 16, 128, 128], F32, kind="ExternalInput")
    wgTD = dt("wgT", [64, 16, 128, 128], F32, kind="ExternalInput")
    wdnTD = dt("wdnT", [16, 64, 128, 128], F32, kind="ExternalInput")
    su_ugD = dt("su_ug", [128, 16], F32, kind="ExternalInput")
    su_dpD = dt("su_dp", [128, 64], F32, kind="ExternalInput")
    wsuD = dt("wsu", [128, 64], F32, kind="ExternalInput")
    wsgD = dt("wsg", [128, 64], F32, kind="ExternalInput")
    wsdD = dt("wsd", [128, 16], F32, kind="ExternalInput")
    lnscD = dt("lnsc", [128, 1], F32, kind="ExternalInput")
    lnbsD = dt("lnbs", [128, 1], F32, kind="ExternalInput")
    h128D = dt("h128", [128, 128], BF16, kind="ExternalInput")
    identbD = dt("identb", [128, 128], BF16, kind="ExternalInput")
    identfD = dt("identf", [128, 128], F32, kind="ExternalInput")
    maskAD = dt("maskA", [128, 128], F32, kind="ExternalInput")
    maskTD = dt("maskT", [128, 128], BF16, kind="ExternalInput")
    onescolD = dt("onescol", [128, 1], BF16, kind="ExternalInput")
    onesrowD = dt("onesrow", [1, 128], BF16, kind="ExternalInput")
    onesrow32D = dt("onesrow32", [1, 128], F32, kind="ExternalInput")
    iota_tD = dt("iota_t", [1, T], F32, kind="ExternalInput")
    iota_jD = dt("iota_j", [128, 1], F32, kind="ExternalInput")
    iota2dD = dt("iota2d", [64, T], F32, kind="ExternalInput")
    k1tabD = dt("k1tab", [64, T], F32, kind="ExternalInput")
    k2tabD = dt("k2tab", [64, T], F32, kind="ExternalInput")

    xs1 = dt("xs1", [D, T], F32)
    xs2 = dt("xs2", [D, T], F32)
    xs3 = dt("xs3", [D, T], F32)
    qf_s = dt("qf_s", [D, T], BF16)
    kf_s = dt("kf_s", [D, T], BF16)
    vt_s = dt("vt_s", [T, D], BF16)
    ao_s = dt("ao_s", [D, T], BF16)
    qkv_s = dt("qkv_s", [T, 3 * D], BF16)
    dbg1 = dt("dbg1", [128, 4096], BF16, kind="ExternalOutput") if os.environ.get("MK_DBG") else None
    dbg2 = dt("dbg2", [128, 16384], BF16, kind="ExternalOutput") if os.environ.get("MK_DBG") else None

    def sbuf(name, shape, dtp):
        return ctx.enter_context(nc.sbuf_tensor(name, shape, dtp))

    AR1 = sbuf("AR1", [128, 65536], BF16)
    AR2 = sbuf("AR2", [128, 16384], BF16)
    AR3 = sbuf("AR3", [128, 8192], BF16)
    AR4 = sbuf("AR4", [128, 4096], F32)
    CON = sbuf("CON", [128, 2688], BF16)
    CONF = sbuf("CONF", [128, 640], F32)
    ROWS = sbuf("ROWS", [128, 1280], F32)

    PQ = [ctx.enter_context(nc.psum_tensor(f"pq{i}", [128, 512], F32))
          for i in range(6)]
    PMB = ctx.enter_context(nc.psum_tensor("pmb", [128, 1024], BF16))
    PMB2 = ctx.enter_context(nc.psum_tensor("pmb2", [128, 1024], BF16))

    p = Prog(nc, ctx)

    c_h128 = CON[:, 0:128]
    c_ident = CON[:, 128:256]
    c_maskT = CON[:, 256:384]
    c_onescol = CON[:, 384:385]
    c_onesrow = CON[0:1, 512:640]
    c_ctab = CON[:, 640:1664]
    c_stab = CON[:, 1664:2688]
    c_identf = CONF[:, 0:128]
    c_maskA = CONF[:, 128:256]
    c_onesrow32 = CONF[0:1, 256:384]
    c_su_ug = CONF[:, 384:400]
    c_sup = CONF[:, 400:464]
    c_sgt = CONF[:, 464:528]
    c_sdn = CONF[:, 528:544]
    c_lnsc = CONF[:, 544:545]
    c_lnbs = CONF[:, 545:546]
    c_rs128 = CONF[:, 546:547]
    c_iota_p = CONF[:, 547:548]
    c_scr0 = CONF[:, 548:549]
    c_scr1 = CONF[:, 549:550]
    c_n1 = CONF[0:1, 550:551]
    c_n2 = CONF[0:1, 551:552]
    c_n3 = CONF[0:1, 552:553]
    c_su_dpv = CONF[:, 553:617]
    c_eps5 = CONF[:, 617:618]
    c_eps8 = CONF[:, 618:619]
    c_r2048c = CONF[:, 619:620]
    c_r8192c = CONF[:, 620:621]
    c_scrS = CONF[:, 621:622]
    c_scrE = CONF[:, 622:623]
    c_invf64 = CONF[0:64, 622:623]
    nc.const_aps.register(F32, 1e-5, c_eps5)
    nc.const_aps.register(F32, 1e-8, c_eps8)

    r_mb = ROWS[:, 0:256]
    r_rb = ROWS[:, 256:512]
    r_mrow = ROWS[0:1, 512:768]
    r_e2row = ROWS[0:1, 768:1024]
    r_tmprow = ROWS[0:1, 1024:1280]
    a_denr = ROWS[0:1, 512:1024]

    def h_ap(ch):
        return AR1[:, ch * 2048:(ch + 1) * 2048]

    def vb_ap(ch):
        return AR1[:, 32768 + ch * 2048:32768 + (ch + 1) * 2048]

    k_ap = h_ap
    ao_ap = h_ap

    def za_ap(ic):
        return AR1[:, ic * 1024:(ic + 1) * 1024]

    def u_ap(ch):
        return AR2[:, ch * 1024:(ch + 1) * 1024]

    def prologue():
        LV = int(os.environ.get("MK_PRO", "9"))
        cl = []
        for dst, src in [
            (c_h128, h128D.ap()), (c_ident, identbD.ap()), (c_maskT, maskTD.ap()),
            (c_onescol, onescolD.ap()), (c_onesrow, onesrowD.ap()),
            (c_identf, identfD.ap()), (c_maskA, maskAD.ap()),
            (c_onesrow32, onesrow32D.ap()), (c_su_ug, su_ugD.ap()),
            (c_su_dpv, su_dpD.ap()), (c_sup, wsuD.ap()), (c_sgt, wsgD.ap()),
            (c_sdn, wsdD.ap()), (c_lnsc, lnscD.ap()), (c_lnbs, lnbsD.ap()),
            (c_iota_p, iota_jD.ap()),
        ]:
            cl.append(p.dma("g", dst, src))
        iota_row = AR2[0:1, 0:4096].bitcast(F32)[0:1, 0:T]
        e_iot = p.dma("s", iota_row, iota_tD.ap())
        if LV < 1:
            return [e_iot]

        def ms_f(e):
            e.memset(c_n1, 128.0)
            e.memset(c_n2, 2048.0)
            e.memset(c_eps5, 1e-5)
            e.memset(c_eps8, 1e-8)
            return e.memset(c_n3, 8192.0)
        e_ms = p.c("pool", ms_f, waits=cl)

        def sc_f(e):
            e.activation(c_n1, c_n1, AF.Ln)
            e.activation(c_n1, c_n1, AF.Exp, scale=-0.5)
            e.activation(c_n2, c_n2, AF.Ln)
            e.activation(c_n2, c_n2, AF.Exp, scale=-0.5)
            e.activation(c_n3, c_n3, AF.Ln)
            return e.activation(c_n3, c_n3, AF.Exp, scale=-0.5)
        e_sc = p.c("act", sc_f, waits=[e_ms])

        def bc_f(e):
            e.matmul(PQ[0][:, 0:1], c_onesrow32, c_n1, start=True, stop=True)
            e.matmul(PQ[0][:, 1:2], c_onesrow32, c_n2, start=True, stop=True)
            return e.matmul(PQ[0][:, 2:3], c_onesrow32, c_n3, start=True, stop=True)
        e_bc = p.c("pe", bc_f, waits=[e_sc])
        def rs_f(e):
            e.activation(c_rs128, PQ[0][:, 0:1], AF.Identity)
            e.activation(c_r2048c, PQ[0][:, 1:2], AF.Identity)
            return e.activation(c_r8192c, PQ[0][:, 2:3], AF.Identity)
        e_rs = p.c("act", rs_f, waits=[e_bc])

        def fold_f(e):
            e.tensor_tensor(c_sup, c_sup, c_r2048c.broadcast_to([128, 64]),
                            ALU.mult)
            e.tensor_tensor(c_sup, c_sup, c_su_dpv, ALU.mult)
            e.tensor_tensor(c_sgt, c_sgt, c_r2048c.broadcast_to([128, 64]),
                            ALU.mult)
            return e.tensor_tensor(c_sdn, c_sdn,
                                   c_r8192c.broadcast_to([128, 16]), ALU.mult)
        e_fold = p.c("dve", fold_f, waits=[e_rs])
        if LV < 2:
            return [e_fold]

        th = AR4[0:64, 0:2048]
        ktab = AR4[0:64, 2048:4096]

        def invf_f(e):
            e.tensor_scalar_mul(c_scr0, c_iota_p, C_HI)
            return e.tensor_scalar_mul(c_scr1, c_iota_p, C_LO)
        e_if = p.c("dve", invf_f, waits=cl)
        e_if2 = p.c("dve", lambda e: e.tensor_add(c_scrS, c_scr0, c_scr1),
                    waits=[e_if, e_fold])

        def ifx_f(e):
            e.activation(c_scrE, c_scrS, AF.Exp)
            return e.activation(c_identf[:, 0:64], c_identf[:, 0:64],
                                AF.Identity)
        e_ifx = p.c("act", ifx_f, waits=[e_if2])

        def sincos(dst_tab, kD, phase_shift, e_war):
            e_t = p.dma("s", th, iota2dD.ap(), waits=[e_war])
            e_k = p.dma("s", ktab, kD.ap(), waits=[e_war])
            if phase_shift:
                e0 = p.c("act", lambda e: e.activation(
                    th, th, AF.Identity, scale=c_invf64), waits=[e_t, e_ifx])
                e0 = p.c("dve", lambda e: e.tensor_scalar(
                    th, th, PIH_HI, PIH_LO, ALU.add, ALU.add), waits=[e0])
            else:
                e0 = p.c("act", lambda e: e.activation(
                    th, th, AF.Identity, scale=c_invf64), waits=[e_t, e_ifx])
            e_r1 = p.c("dve", lambda e: e.scalar_tensor_tensor(
                th, ktab, -TPI_HI, th, ALU.mult, ALU.add), waits=[e0, e_k])
            e_r2 = p.c("dve", lambda e: e.scalar_tensor_tensor(
                th, ktab, -TPI_LO, th, ALU.mult, ALU.add), waits=[e_r1])

            def clamp_f(e):
                e.tensor_scalar_min(th, th, PI_F32)
                return e.tensor_scalar_max(th, th, -PI_F32)
            e_cl = p.c("dve", clamp_f, waits=[e_r2])
            e_sin = p.c("act", lambda e: e.activation(th, th, AF.Sin),
                        waits=[e_cl])
            e_ev = None
            for tt in range(16):
                e_tr = p.c("pe", lambda e, tt=tt: e.transpose(
                    PQ[3][:, 0:64], th[0:64, tt * 128:(tt + 1) * 128],
                    c_identf[0:64, 0:64]), waits=[e_sin, e_ev])
                e_ev = p.c("act", lambda e, tt=tt: e.activation(
                    dst_tab[:, tt * 64:(tt + 1) * 64], PQ[3][:, 0:64],
                    AF.Identity), waits=[e_tr])
            return e_ev

        e_stab = sincos(c_stab, k1tabD, False, e_rs)
        e_ctab = sincos(c_ctab, k2tabD, True, e_stab)
        return [e_fold, e_ctab]

    ln_ps = {"pq5": None, "pq6": None}

    def ln_phase(src_dram, h_dst_fn, su_fold, tok_lo, tok_hi, h_war=()):
        xmini = AR4[:, 0:SEG * 16].rearrange("p (c t) -> p c t", c=16)
        xb = AR3[:, 0:SEG * 16].rearrange("p (c t) -> p c t", c=16)
        e_h = list(h_war)
        e_s2 = None
        e_bce = None
        for g in range((tok_hi - tok_lo) // SEG):
            lo = tok_lo + g * SEG
            ld = p.dma("s", xmini[:, :, :],
                       src_dram.ap().rearrange("(c p) t -> p c t", p=128)
                       [:, :, lo:lo + SEG],
                       waits=e_h[-2:])

            def cast_f(e):
                last = None
                for ch in range(16):
                    last = e.tensor_copy(xb[:, ch, :], xmini[:, ch, :])
                return last
            e_cast = p.c("dve", cast_f, waits=[ld, e_s2])

            def s1_f(e):
                last = None
                for ch in range(16):
                    last = e.matmul(PQ[4][0:1, 0:SEG], c_onescol, xb[:, ch, :],
                                    start=(ch == 0), stop=(ch == 15))
                return last
            e_s1 = p.c("pe", s1_f, waits=[e_cast, ln_ps["pq5"]])

            def sq_f(e):
                last = None
                for ch in range(16):
                    last = e.activation(xb[:, ch, :], xb[:, ch, :], AF.Square)
                return last
            e_sq = p.c("act", sq_f, waits=[e_s1])

            def s2_f(e):
                last = None
                for ch in range(16):
                    last = e.matmul(PQ[5][0:1, 0:SEG], c_onescol, xb[:, ch, :],
                                    start=(ch == 0), stop=(ch == 15))
                return last
            e_s2 = p.c("pe", s2_f, waits=[e_sq, ln_ps["pq6"]])

            def rows_f(e):
                e.activation(r_mrow, PQ[4][0:1, 0:SEG], AF.Identity,
                             scale=1.0 / 2048.0)
                return e.activation(r_e2row, PQ[5][0:1, 0:SEG], AF.Identity,
                                    scale=1.0 / 2048.0)
            e_rows = p.c("act", rows_f, waits=[e_s1, e_s2])

            def var_f(e):
                e.tensor_tensor(r_tmprow, r_mrow, r_mrow, ALU.mult)
                return e.tensor_tensor(r_e2row, r_e2row, r_tmprow, ALU.subtract)
            e_var = p.c("dve", var_f, waits=[e_rows])

            def rstd_f(e):
                e.activation(r_e2row, r_e2row, AF.Ln, bias=1e-5)
                return e.activation(r_e2row, r_e2row, AF.Exp, scale=-0.5)
            e_rstd = p.c("act", rstd_f, waits=[e_var])

            def bc_f(e):
                e.matmul(PQ[4][:, 0:SEG], c_onesrow32, r_mrow,
                         start=True, stop=True)
                return e.matmul(PQ[5][:, 0:SEG], c_onesrow32, r_e2row,
                                start=True, stop=True)
            e_bc = p.c("pe", bc_f, waits=[e_rstd])

            def bce_f(e):
                e.activation(r_mb, PQ[4][:, 0:SEG], AF.Identity)
                return e.activation(r_rb, PQ[5][:, 0:SEG], AF.Identity)
            e_bce = p.c("act", bce_f, waits=[e_bc] + e_h[-2:])
            ln_ps["pq5"] = e_bce
            ln_ps["pq6"] = e_bce

            def mk_h(chs, lo=None, use_stt=False):
                def h_f(e, lo=lo):
                    last = None
                    for ch in chs:
                        dst = h_dst_fn(ch, lo, SEG)
                        e.tensor_tensor(xmini[:, ch, :], xmini[:, ch, :], r_mb,
                                        ALU.subtract)
                        if su_fold and use_stt:
                            last = e.scalar_tensor_tensor(
                                dst, xmini[:, ch, :], c_su_ug[:, ch:ch + 1],
                                r_rb, ALU.mult, ALU.mult)
                        else:
                            last = e.tensor_tensor(dst, xmini[:, ch, :], r_rb,
                                                   ALU.mult)
                    return last
                return h_f
            e_h1 = p.c("dve", mk_h(range(0, 8), lo, True), waits=[e_bce])
            e_h2 = p.c("pool", mk_h(range(8, 16), lo), waits=[e_bce])
            if su_fold:
                def su_f(e, lo=lo):
                    last = None
                    for ch in range(8, 16):
                        dst = h_dst_fn(ch, lo, SEG)
                        last = e.tensor_scalar_mul(dst, dst,
                                                   c_su_ug[:, ch:ch + 1])
                    return last
                e_h2 = p.c("dve", su_f, waits=[e_h2])
            e_h.append(e_h1)
            e_h.append(e_h2)
        return e_h[-2:]

    def qk_proj(wD, dst_dram, h_evts, pst):
        e_cons = {-2: pst.get("cons"), -1: pst.get("cons")}
        e_quad = pst.setdefault("quad", {})
        e_stage = pst.setdefault("stage", {})
        last = None
        for sl in range(8):
            buf = AR3[:, (sl % 2) * 4096:(sl % 2) * 4096 + 4096]
            bufv = buf.rearrange("p (jc hc j) -> p jc hc j", jc=2, hc=16)
            ld = p.dma("g", bufv,
                       wD.ap()[2 * sl:2 * sl + 2].rearrange(
                           "jc hc hp j -> hp jc hc j"),
                       waits=[e_cons.get(sl - 2)])
            for jc2 in range(2):
                jch = 2 * sl + jc2
                for tq in range(4):
                    qi = (jc2 * 4 + tq) % 4
                    qd = PQ[qi]

                    def mm_f(e, bufv=bufv, jc2=jc2, tq=tq, qd=qd):
                        ll = None
                        for hc in range(16):
                            ll = e.matmul(
                                qd[:, :], bufv[:, jc2, hc, :],
                                h_ap(hc)[:, tq * 512:(tq + 1) * 512],
                                start=(hc == 0), stop=(hc == 15))
                        return ll
                    e_mm = p.c("pe", mm_f,
                               waits=[ld, e_quad.get(qi)] + list(h_evts))
                    e_cons[sl] = e_mm
                    stg = AR2[:, qi * 512:(qi + 1) * 512]
                    e_ev = p.c("act", lambda e, qd=qd, stg=stg: e.activation(
                        stg, qd[:, :], AF.Identity),
                        waits=[e_mm, e_stage.get(qi)])
                    e_quad[qi] = e_ev
                    e_st = p.dma("s",
                                 dst_dram.ap()[jch * 128:(jch + 1) * 128,
                                               tq * 512:(tq + 1) * 512],
                                 stg, waits=[e_ev])
                    e_stage[qi] = e_st
                    last = e_st
        pst["cons"] = e_cons[7]
        return last

    def v_proj(h_evts, pst):
        lds = []
        for i in range(4):
            lds.append(p.dma(
                "g",
                AR1[:, 32768 + i * 8192:32768 + (i + 1) * 8192].rearrange(
                    "p (c j) -> p c j", c=4),
                wvrD.ap()[4 * i:4 * i + 4].rearrange("c hp j -> hp c j")))
        e_quad = pst.setdefault("quad", {})
        e_stage = pst.setdefault("stage", {})
        last = None
        for tt in range(16):
            for js in range(4):
                qi = (tt * 4 + js) % 4
                qd = PQ[qi]

                def mm_f(e, tt=tt, js=js, qd=qd):
                    ll = None
                    for hc in range(16):
                        ll = e.matmul(
                            qd[:, :], h_ap(hc)[:, tt * 128:(tt + 1) * 128],
                            vb_ap(hc)[:, js * 512:(js + 1) * 512],
                            start=(hc == 0), stop=(hc == 15))
                    return ll
                e_mm = p.c("pe", mm_f,
                           waits=[lds[-1], e_quad.get(qi)] + list(h_evts))
                stg = AR2[:, qi * 512:(qi + 1) * 512]
                e_ev = p.c("act", lambda e, qd=qd, stg=stg: e.activation(
                    stg, qd[:, :], AF.Identity), waits=[e_mm, e_stage.get(qi)])
                e_quad[qi] = e_ev
                e_st = p.dma("s", vt_s.ap()[tt * 128:(tt + 1) * 128,
                                            js * 512:(js + 1) * 512],
                             stg, waits=[e_ev])
                e_stage[qi] = e_st
                last = e_st
        return last

    def attn_core(kind, qfD, kfD, vD, vcol0, eps):
        def qs_ap(h):
            return AR2[:, h * 512:(h + 1) * 512]

        def aout_ap(h):
            return AR2[:, 8192 + h * 512:8192 + (h + 1) * 512]

        p_t = [AR3[:, 0:512], AR3[:, 512:1024]]
        negm_bf = AR3[0:1, 1024:1536]
        sq_t = AR4[:, 0:512]
        rden_b = AR4[:, 512:1024]
        mcols = AR4[:, 1024:1028]
        mtmp = AR4[:, 1028:1029]

        e_aout_store = None
        e_psum = {}
        e_p_free = {}
        last = None
        for s in range(4):
            ev_k = p.dma(
                "s", AR1[:, 0:32768].rearrange("p (h c) -> p h c", h=16)
                [:, :, s * 512:(s + 1) * 512],
                kfD.ap().rearrange("(h p) t -> p h t", p=128)
                [:, :, s * 512:(s + 1) * 512])
            ev_v = p.dma(
                "s", AR1[:, 32768:65536].rearrange("p (c j) -> p c j", c=16)
                [:, 4 * s:4 * s + 4, :],
                vD.ap().rearrange("(c p) j -> p c j", p=128)
                [:, 4 * s:4 * s + 4, vcol0:vcol0 + 2048])
            ev_q = p.dma(
                "s", AR2[:, 0:8192].rearrange("p (h c) -> p h c", h=16),
                qfD.ap().rearrange("(h p) t -> p h t", p=128)
                [:, :, s * 512:(s + 1) * 512],
                waits=[e_aout_store])
            base_w = [ev_k, ev_v, ev_q]
            nkc = 4 * (s + 1)
            for h in range(16):
                e_negm = None
                if kind == "soft":
                    e_red = None
                    for tau in range(4):
                        L = s * 512 + (tau + 1) * 128
                        nkb = (L + 511) // 512
                        for kb in range(nkb):
                            n = min(512, L - kb * 512)
                            qd = PQ[kb % 2]
                            e_mm = p.c("pe", lambda e, h=h, tau=tau, kb=kb, n=n,
                                       qd=qd: e.matmul(
                                           qd[:, 0:n],
                                           qs_ap(h)[:, tau * 128:(tau + 1) * 128],
                                           k_ap(h)[:, kb * 512:kb * 512 + n],
                                           start=True, stop=True),
                                       waits=base_w + [e_psum.get(kb % 2)])

                            def red_f(e, qd=qd, n=n, kb=kb, tau=tau, nkb=nkb):
                                if kb == nkb - 1:
                                    e.tensor_tensor(qd[:, n - 128:n],
                                                    qd[:, n - 128:n],
                                                    c_maskA, ALU.add)
                                e.reduce_max(mtmp, qd[:, 0:n], axis=AX.X)
                                if kb == 0:
                                    return e.tensor_copy(mcols[:, tau:tau + 1],
                                                         mtmp)
                                return e.tensor_tensor(
                                    mcols[:, tau:tau + 1],
                                    mcols[:, tau:tau + 1], mtmp, ALU.max)
                            e_red = p.c("dve", red_f, waits=[e_mm, e_red])
                            e_psum[kb % 2] = e_red
                    e_tr = None
                    for tau in range(4):
                        e_tr = p.c("pe", lambda e, tau=tau: e.transpose(
                            PQ[4][0:1, tau * 128:(tau + 1) * 128],
                            mcols[:, tau:tau + 1], c_identf),
                            waits=[e_red, e_psum.get(4), e_tr])
                    e_negm = p.c("act", lambda e: e.activation(
                        negm_bf, PQ[4][0:1, 0:512], AF.Identity, scale=-1.0),
                        waits=[e_tr])
                    e_psum[4] = e_negm

                e_exp = None
                e_av = None
                e_den = None
                for ki, kc in enumerate(range(nkc - 1, -1, -1)):
                    first, lastk = (ki == 0), (kc == 0)
                    lo = max(0, kc - 4 * s) * 128
                    n = 512 - lo
                    si = 2 + (kc % 2)
                    qd = PQ[si]

                    def st_f(e, h=h, kc=kc, lo=lo, n=n, qd=qd, kind=kind):
                        mm = e.matmul(qd[:, lo:lo + n],
                                      k_ap(h)[:, kc * 128:(kc + 1) * 128],
                                      qs_ap(h)[:, lo:lo + n],
                                      start=True, stop=(kind != "soft"))
                        if kind == "soft":
                            mm = e.matmul(qd[:, lo:lo + n], c_onesrow,
                                          negm_bf[0:1, lo:lo + n],
                                          start=False, stop=True)
                        return mm
                    e_st = p.c("pe", st_f,
                               waits=base_w + [e_psum.get(si), e_negm])

                    pt = p_t[kc % 2]
                    if kind == "soft":
                        e_exp = p.c("act", lambda e, qd=qd, lo=lo, n=n, pt=pt:
                                    e.activation(pt[:, lo:lo + n],
                                                 qd[:, lo:lo + n], AF.Exp,
                                                 scale=c_rs128),
                                    waits=[e_st, e_p_free.get(kc % 2), e_exp])
                    else:
                        e_sq1 = p.c("act", lambda e, qd=qd, lo=lo, n=n:
                                    e.activation(sq_t[:, lo:lo + n],
                                                 qd[:, lo:lo + n], AF.Square),
                                    waits=[e_st, e_exp])
                        e_exp = p.c("act", lambda e, lo=lo, n=n, pt=pt:
                                    e.activation(pt[:, lo:lo + n],
                                                 sq_t[:, lo:lo + n], AF.Square),
                                    waits=[e_sq1, e_p_free.get(kc % 2)])
                    e_psum[si] = e_exp
                    if kc >= 4 * s:
                        e_exp = p.c("dve", lambda e, pt=pt, lo=lo:
                                    e.tensor_tensor(pt[:, lo:lo + 128],
                                                    pt[:, lo:lo + 128],
                                                    c_maskT, ALU.mult),
                                    waits=[e_exp])
                        if lo > 0:
                            e_exp = p.c("pool", lambda e, pt=pt, lo=lo:
                                        e.memset(pt[:, 0:lo], 0.0),
                                        waits=[e_exp, e_p_free.get(kc % 2)])

                    e_av = p.c("pe", lambda e, kc=kc, h=h, pt=pt,
                               first=first, lastk=lastk:
                               e.matmul(PQ[5][:, 0:512],
                                        vb_ap(kc)[:, h * 128:(h + 1) * 128],
                                        pt[:, 0:512],
                                        start=first, stop=lastk),
                               waits=[e_exp] + ([e_psum.get(5)] if first else []))
                    e_den = p.c("pe", lambda e, kc=kc, pt=pt,
                                first=first, lastk=lastk:
                                e.matmul(PQ[4][0:1, 0:512], c_onescol,
                                         pt[:, 0:512],
                                         start=first, stop=lastk),
                                waits=([e_psum.get(4)] if first else []))
                    e_p_free[kc % 2] = e_den

                def div_f(e, eps=eps):
                    if eps:
                        e.activation(a_denr, PQ[4][0:1, 0:512], AF.Ln, bias=eps)
                    else:
                        e.activation(a_denr, PQ[4][0:1, 0:512], AF.Ln)
                    return e.activation(a_denr, a_denr, AF.Exp, scale=-1.0)
                e_div = p.c("act", div_f, waits=[e_den])
                e_psum[4] = e_div
                e_rbc = p.c("pe", lambda e: e.matmul(
                    PQ[4][:, 0:512], c_onesrow32, a_denr, start=True, stop=True),
                    waits=[e_div, e_psum.get(4)])
                e_rbe = p.c("act", lambda e: e.activation(
                    rden_b, PQ[4][:, 0:512], AF.Identity), waits=[e_rbc])
                e_psum[4] = e_rbe
                e_out = p.c("dve", lambda e, h=h: e.tensor_tensor(
                    aout_ap(h), PQ[5][:, 0:512], rden_b, ALU.mult),
                    waits=[e_rbe, e_av, e_aout_store])
                e_psum[5] = e_out
                last = e_out
            e_aout_store = p.dma(
                "s", ao_s.ap().rearrange("(h p) t -> p h t", p=128)
                [:, :, s * 512:(s + 1) * 512],
                AR2[:, 8192:16384].rearrange("p (h c) -> p h c", h=16),
                waits=[last])
        return e_aout_store

    def out_proj(wD, xin_dram, xout_dram, e_ao_ready):
        lds = []
        for i in range(4):
            lds.append(p.dma(
                "s", AR1[:, i * 8192:(i + 1) * 8192].rearrange(
                    "p (c t) -> p c t", c=4),
                ao_s.ap().rearrange("(c p) t -> p c t", p=128)
                [:, 4 * i:4 * i + 4, :],
                waits=[e_ao_ready]))
        e_cons = {}
        e_quad = {}
        e_xin = {}
        e_xout = {}
        last = None
        for sl in range(8):
            buf = AR3[:, (sl % 2) * 4096:(sl % 2) * 4096 + 4096]
            bufv = buf.rearrange("p (oc jc o) -> p oc jc o", oc=2, jc=16)
            ld = p.dma("g", bufv,
                       wD.ap()[2 * sl:2 * sl + 2].rearrange(
                           "oc jc jp o -> jp oc jc o"),
                       waits=[e_cons.get(sl - 2)])
            for oc2 in range(2):
                oc = 2 * sl + oc2
                for tq in range(4):
                    qi = (oc2 * 4 + tq) % 4
                    qd = PQ[qi]
                    xi = (oc2 * 4 + tq) % 2
                    xin_t = AR4[:, xi * 512:(xi + 1) * 512]
                    xout_t = AR4[:, 1024 + xi * 512:1024 + (xi + 1) * 512]
                    e_xl = p.dma("s", xin_t,
                                 xin_dram.ap()[oc * 128:(oc + 1) * 128,
                                               tq * 512:(tq + 1) * 512],
                                 waits=[e_xin.get(xi)])

                    def mm_f(e, bufv=bufv, oc2=oc2, tq=tq, qd=qd):
                        ll = None
                        for jc in range(16):
                            ll = e.matmul(qd[:, :], bufv[:, oc2, jc, :],
                                          ao_ap(jc)[:, tq * 512:(tq + 1) * 512],
                                          start=(jc == 0), stop=(jc == 15))
                        return ll
                    e_mm = p.c("pe", mm_f, waits=[ld, lds[-1], e_quad.get(qi)])
                    e_cons[sl] = e_mm
                    e_add = p.c("dve", lambda e, qd=qd, xin_t=xin_t,
                                xout_t=xout_t: e.tensor_tensor(
                                    xout_t, qd[:, :], xin_t, ALU.add),
                                waits=[e_mm, e_xl, e_xout.get(xi)])
                    e_quad[qi] = e_add
                    e_xin[xi] = e_add
                    e_st = p.dma("s",
                                 xout_dram.ap()[oc * 128:(oc + 1) * 128,
                                                tq * 512:(tq + 1) * 512],
                                 xout_t, waits=[e_add])
                    e_xout[xi] = e_st
                    last = e_st
        return last

    def ffn(xin_dram, xout_dram):
        ustg_d = AR1[:, 0:1024]
        ustg_p = AR1[:, 1024:2048]
        ztg_d = AR4[:, 2048:2560].bitcast(BF16)
        ztg_p = AR4[:, 2560:3072].bitcast(BF16)
        last = None
        for half in range(2):
            t0 = half * 1024
            htmp = AR3[:, 4096:8192].rearrange("p (c t) -> p c t", c=16)
            e_sa_all = []
            e_sa_last = []
            e_ubf_parts = []
            for g in range(1024 // SEG):
                lo = t0 + g * SEG
                evs = ln_phase(xin_dram, lambda ch, l, s_: htmp[:, ch, :],
                               True, lo, lo + SEG, h_war=e_sa_last)
                e_sa_last = []
                for chp in range(8):
                    qd = PQ[chp % 2]

                    def samm_f(e, chp=chp, qd=qd):
                        e.matmul(qd[:, 0:SEG], c_h128, htmp[:, 2 * chp, :],
                                 start=True, stop=True)
                        return e.matmul(qd[:, SEG:2 * SEG], c_h128,
                                        htmp[:, 2 * chp + 1, :],
                                        start=True, stop=True)
                    e_mm = p.c("pe", samm_f,
                               waits=evs + [e_sa_all[-2] if len(e_sa_all) >= 2
                                            else None])

                    def saev_f(e, chp=chp, qd=qd, g=g):
                        e.activation(u_ap(2 * chp)[:, g * SEG:(g + 1) * SEG],
                                     qd[:, 0:SEG], AF.Identity)
                        return e.activation(
                            u_ap(2 * chp + 1)[:, g * SEG:(g + 1) * SEG],
                            qd[:, SEG:2 * SEG], AF.Identity)
                    e_ev = p.c("act", saev_f, waits=[e_mm])
                    e_sa_all.append(e_ev)
                    e_sa_last.append(e_mm)
                if dbg1 is not None and half == 1 and g == 3:
                    p.dma("s", dbg1.ap(), AR3[:, 4096:8192], waits=e_sa_last[-2:])
            e_tap = None
            if dbg2 is not None and half == 1:
                e_tap = p.dma("s", dbg2.ap(), AR2[:, 0:16384], waits=e_sa_all[-2:])
            e_bf = e_sa_all[-2:] + ([e_tap] if dbg2 is not None else [])
            for span in (1, 2, 4, 8):
                nxt = []
                pairs = [b for b in range(16) if not (b & span)]
                for i, b in enumerate(pairs):
                    eng = "dve" if i % 2 == 0 else "pool"
                    tt_ = ustg_d if i % 2 == 0 else ustg_p

                    def bf_f(e, a=b, c=b + span, tt_=tt_):
                        e.tensor_tensor(tt_, u_ap(a), u_ap(c), ALU.subtract)
                        e.tensor_tensor(u_ap(a), u_ap(a), u_ap(c), ALU.add)
                        return e.tensor_copy(u_ap(c), tt_)
                    nxt.append(p.c(eng, bf_f, waits=e_bf))
                e_bf = nxt[-2:]
            e_ubf = e_bf

            e_w = {}
            e_ze = None
            z_evt = {}
            for ic in range(64):
                ub = AR3[:, 4096 + (ic % 2) * 2048:4096 + (ic % 2) * 2048 + 2048]
                gb = AR3[:, (ic % 2) * 2048:(ic % 2) * 2048 + 2048]
                ubv = ub.rearrange("p (hc j) -> p hc j", hc=16)
                gbv = gb.rearrange("p (hc j) -> p hc j", hc=16)
                lu = p.dma("g", ubv,
                           wupTD.ap()[ic].rearrange("hc hp j -> hp hc j"),
                           waits=[e_w.get(("ucons", ic - 2))]
                           + (e_ubf if ic < 2 else []))
                lg = p.dma("g", gbv,
                           wgTD.ap()[ic].rearrange("hc hp j -> hp hc j"),
                           waits=[e_w.get(("gcons", ic - 2))]
                           + (e_ubf if ic < 2 else []))
                for tqh in range(2):
                    qu = PQ[tqh]
                    qg = PQ[2 + tqh]

                    def up_f(e, ubv=ubv, tqh=tqh, qd=qu):
                        ll = None
                        for hc in range(16):
                            ll = e.matmul(qd[:, :], ubv[:, hc, :],
                                          u_ap(hc)[:, tqh * 512:(tqh + 1) * 512],
                                          start=(hc == 0), stop=(hc == 15))
                        return ll
                    e_up = p.c("pe", up_f,
                               waits=[lu, e_w.get(("psu", tqh))] + e_ubf)
                    e_w[("ucons", ic)] = e_up

                    def gt_f(e, gbv=gbv, tqh=tqh, qd=qg):
                        ll = None
                        for hc in range(16):
                            ll = e.matmul(qd[:, :], gbv[:, hc, :],
                                          u_ap(hc)[:, tqh * 512:(tqh + 1) * 512],
                                          start=(hc == 0), stop=(hc == 15))
                        return ll
                    e_gt = p.c("pe", gt_f,
                               waits=[lg, e_w.get(("psg", tqh))] + e_ubf)
                    e_w[("gcons", ic)] = e_gt

                    sg = AR4[:, tqh * 512:(tqh + 1) * 512]
                    e_si = p.c("act", lambda e, qd=qg, sg=sg, ic=ic:
                               e.activation(sg, qd[:, :], AF.Silu,
                                            scale=c_sgt[:, ic:ic + 1]),
                               waits=[e_gt, e_w.get(("sg", tqh))])
                    e_w[("psg", tqh)] = e_si
                    zt = AR4[:, 1024 + tqh * 256:1024 + (tqh + 1) * 256].bitcast(BF16)
                    e_z = p.c("dve", lambda e, qd=qu, sg=sg, zt=zt, ic=ic:
                              e.scalar_tensor_tensor(zt, qd[:, :],
                                                     c_sup[:, ic:ic + 1], sg,
                                                     ALU.mult, ALU.mult),
                              waits=[e_si, e_up, e_w.get(("zt", tqh))])
                    e_w[("psu", tqh)] = e_z
                    e_w[("sg", tqh)] = e_z
                    e_sa = p.c("pe", lambda e, zt=zt: e.matmul(
                        PQ[4][:, :], c_h128, zt, start=True, stop=True),
                        waits=[e_z, e_ze])
                    e_ze = p.c("act", lambda e, ic=ic, tqh=tqh: e.activation(
                        za_ap(ic)[:, tqh * 512:(tqh + 1) * 512], PQ[4][:, :],
                        AF.Identity), waits=[e_sa])
                    e_w[("zt", tqh)] = e_sa
                    e_w[("lastz", tqh)] = e_z
                z_evt[ic] = e_ze
                for s in (1, 2, 4, 8, 16):
                    if (ic + 1) % (2 * s) == 0:
                        for b in range(ic + 1 - 2 * s, ic + 1 - s):
                            def zbf_f(e, a=b, c=b + s):
                                e.tensor_tensor(ztg_d, za_ap(a), za_ap(c),
                                                ALU.subtract)
                                e.tensor_tensor(za_ap(a), za_ap(a), za_ap(c),
                                                ALU.add)
                                return e.tensor_copy(za_ap(c), ztg_d)
                            ev = p.c("dve", zbf_f,
                                     waits=[z_evt[b], z_evt[b + s]])
                            z_evt[b] = ev
                            z_evt[b + s] = ev
            zfin = []
            for i, b in enumerate(range(32)):
                eng = "dve" if i % 2 == 0 else "pool"
                tt_ = ztg_d if i % 2 == 0 else ztg_p

                def bf_f(e, a=b, c=b + 32, tt_=tt_):
                    e.tensor_tensor(tt_, za_ap(a), za_ap(c), ALU.subtract)
                    e.tensor_tensor(za_ap(a), za_ap(a), za_ap(c), ALU.add)
                    return e.tensor_copy(za_ap(c), tt_)
                ev = p.c(eng, bf_f, waits=[z_evt[b], z_evt[b + 32]])
                z_evt[b] = ev
                z_evt[b + 32] = ev
                zfin.append(ev)
            e_bf = zfin[-2:]

            e_parts = {}
            e_ps = {}
            e_x = {}
            for oc in range(16):
                bvs = []
                for hh in range(2):
                    b = AR3[:, 4096 * hh:4096 * hh + 4096]
                    bv = b.rearrange("p (ic o) -> p ic o", ic=32)
                    ld = p.dma("g", bv,
                               wdnTD.ap()[oc, 32 * hh:32 * hh + 32].rearrange(
                                   "ic ip o -> ip ic o"),
                               waits=[e_parts.get((oc - 1, hh, 0)),
                                      e_parts.get((oc - 1, hh, 1))]
                               + (e_bf if oc < 2 else []))
                    bvs.append((bv, ld))
                for tqh in range(2):
                    qd = PQ[tqh]
                    xin_t = AR4[:, tqh * 512:(tqh + 1) * 512]
                    xout_t = AR4[:, 1024 + tqh * 512:1024 + (tqh + 1) * 512]
                    e_xl = p.dma("s", xin_t,
                                 xin_dram.ap()[oc * 128:(oc + 1) * 128,
                                               t0 + tqh * 512:t0 + (tqh + 1) * 512],
                                 waits=[e_x.get(("in", tqh)),
                                        e_w.get(("lastz", 0)) if oc == 0 else None,
                                        e_w.get(("lastz", 1)) if oc == 0 else None])

                    def dnA_f(e, bv=bvs[0][0], tqh=tqh, qd=qd):
                        ll = None
                        for ic in range(32):
                            ll = e.matmul(qd[:, :], bv[:, ic, :],
                                          za_ap(ic)[:, tqh * 512:(tqh + 1) * 512],
                                          start=(ic == 0), stop=False)
                        return ll
                    e_A = p.c("pe", dnA_f,
                              waits=[bvs[0][1], e_ps.get(tqh)] + e_bf)
                    e_parts[(oc, 0, tqh)] = e_A

                    def dnB_f(e, bv=bvs[1][0], tqh=tqh, qd=qd):
                        ll = None
                        for ic in range(32):
                            ll = e.matmul(qd[:, :], bv[:, ic, :],
                                          za_ap(32 + ic)[:, tqh * 512:(tqh + 1) * 512],
                                          start=False, stop=(ic == 31))
                        return ll
                    e_B = p.c("pe", dnB_f, waits=[bvs[1][1]])
                    e_parts[(oc, 1, tqh)] = e_B
                    e_ev = p.c("dve", lambda e, qd=qd, oc=oc, xin_t=xin_t,
                               xout_t=xout_t: e.scalar_tensor_tensor(
                                   xout_t, qd[:, :], c_sdn[:, oc:oc + 1],
                                   xin_t, ALU.mult, ALU.add),
                               waits=[e_B, e_xl, e_x.get(("out", tqh))])
                    e_ps[tqh] = e_ev
                    e_x[("in", tqh)] = e_ev
                    e_st = p.dma("s",
                                 xout_dram.ap()[oc * 128:(oc + 1) * 128,
                                                t0 + tqh * 512:t0 + (tqh + 1) * 512],
                                 xout_t, waits=[e_ev])
                    e_x[("out", tqh)] = e_st
                    last = e_st
            p.barrier()
        return last

    def qkv_proj_b(h_evts):
        e_cons = {}
        e_quad = {}
        e_stg = {}
        last = None
        for js in range(24):
            buf = AR3[:, (js % 2) * 4096:(js % 2) * 4096 + 4096]
            bufv = buf.rearrange("p (hc j) -> p hc j", hc=16)
            ld = p.dma("g", bufv,
                       wqkvD.ap().rearrange("hc hp j -> hp hc j")
                       [:, :, js * 256:(js + 1) * 256],
                       waits=[e_cons.get(js - 2)])
            for tg in range(4):
                stg = AR2[:, (tg % 2) * 1024:(tg % 2) * 1024 + 1024]
                e_ev = None
                for ti in range(4):
                    tt = tg * 4 + ti
                    qi = tt % 4
                    qd = PQ[qi]

                    def mm_f(e, bufv=bufv, tt=tt, qd=qd):
                        ll = None
                        for hc in range(16):
                            ll = e.matmul(qd[:, 0:256],
                                          h_ap(hc)[:, tt * 128:(tt + 1) * 128],
                                          bufv[:, hc, :],
                                          start=(hc == 0), stop=(hc == 15))
                        return ll
                    e_mm = p.c("pe", mm_f,
                               waits=[ld, e_quad.get(qi)] + list(h_evts))
                    e_cons[js] = e_mm
                    e_ev = p.c("act", lambda e, qd=qd, stg=stg, ti=ti:
                               e.activation(stg[:, ti * 256:(ti + 1) * 256],
                                            qd[:, 0:256], AF.Identity),
                               waits=[e_mm, e_stg.get(tg % 2)])
                    e_quad[qi] = e_ev
                e_st = p.dma("s",
                             qkv_s.ap().rearrange("(q p) j -> p q j", p=128)
                             [:, 4 * tg:4 * tg + 4, js * 256:(js + 1) * 256],
                             stg.rearrange("p (q j) -> p q j", q=4),
                             waits=[e_ev])
                e_stg[tg % 2] = e_st
                last = e_st
        return last

    def rope_ln_b():
        B2LV = int(os.environ.get("MK_B2", "9"))
        qn = AR4[:, 0:2048]
        qn3 = qn.rearrange("p (h two j) -> p h two j", h=16, two=2)
        qnh = qn.rearrange("p (h j) -> p h j", h=16)
        tmp = AR4[:, 2048:3072]
        tm3 = tmp.rearrange("p (h j) -> p h j", h=16)
        tmh = tmp.rearrange("p (h j) -> p h j", h=8)
        s1 = AR4[:, 3072:3088]
        s2 = AR4[:, 3088:3104]
        msq = AR4[:, 3104:3120]
        last = None
        e_app = None
        e_qb_free = {}
        e_slot = {}
        for tt in range(16):
            for which, colbase, dstD in (("q", 0, qf_s), ("k", 2048, kf_s)):
                tile = AR2[:, 0:2048] if which == "q" else AR2[:, 2048:4096]
                ld = p.dma("s", tile,
                           qkv_s.ap()[tt * 128:(tt + 1) * 128,
                                      colbase:colbase + 2048],
                           waits=[e_app])
                t3 = tile.rearrange("p (h two j) -> p h two j", h=16, two=2)
                q1v, q2v = t3[:, :, 0, :], t3[:, :, 1, :]
                qn1v, qn2v = qn3[:, :, 0, :], qn3[:, :, 1, :]
                cb = c_ctab[:, tt * 64:(tt + 1) * 64].rearrange(
                    "p (o j) -> p o j", o=1).broadcast_to([128, 16, 64])
                sbt = c_stab[:, tt * 64:(tt + 1) * 64].rearrange(
                    "p (o j) -> p o j", o=1).broadcast_to([128, 16, 64])

                def rope_f(e, q1v=q1v, q2v=q2v, qn1v=qn1v, qn2v=qn2v,
                           cb=cb, sbt=sbt):
                    tv = tm3[:, :, 0:64]
                    e.tensor_tensor(qn1v, q1v, cb, ALU.mult)
                    e.tensor_tensor(tv, q2v, sbt, ALU.mult)
                    e.tensor_tensor(qn1v, qn1v, tv, ALU.subtract)
                    e.tensor_tensor(qn2v, q1v, sbt, ALU.mult)
                    e.tensor_tensor(tv, q2v, cb, ALU.mult)
                    return e.tensor_tensor(qn2v, qn2v, tv, ALU.add)
                if B2LV < 2:
                    e_app = ld
                    continue
                e_rope = p.c("dve", rope_f, waits=[ld, e_app])
                if B2LV < 3:
                    e_app = e_rope
                    continue

                e_st1 = p.c("dve", lambda e: e.reduce_sum(s1, qnh, axis=AX.X),
                            waits=[e_rope])
                e_sqa = p.c("act", lambda e: e.activation(
                    tmp, qn[:, 0:1024], AF.Square), waits=[e_rope])
                e_r1 = p.c("dve", lambda e: e.reduce_sum(
                    s2[:, 0:8], tmh, axis=AX.X), waits=[e_sqa])
                e_sqb = p.c("act", lambda e: e.activation(
                    tmp, qn[:, 1024:2048], AF.Square), waits=[e_r1, e_sqa])
                e_r2 = p.c("dve", lambda e: e.reduce_sum(
                    s2[:, 8:16], tmh, axis=AX.X), waits=[e_sqb])

                def stat_f(e):
                    e.tensor_scalar_mul(s1, s1, 1.0 / 128.0)
                    e.tensor_scalar_add(tmp[:, 0:512], tmp[:, 0:512], 0.0)
                    e.tensor_scalar_mul(s2, s2, 1.0 / 128.0)
                    e.tensor_tensor(msq, s1, s1, ALU.mult)
                    e.tensor_scalar_add(tmp[:, 512:1024], tmp[:, 512:1024],
                                        0.0)
                    return e.tensor_tensor(s2, s2, msq, ALU.subtract)
                e_stat = p.c("dve", stat_f, waits=[e_r1, e_r2, e_st1])

                def rstd_f(e):
                    e.activation(s2, s2, AF.Ln, bias=1e-5)
                    return e.activation(s2, s2, AF.Exp, scale=-0.5)
                e_rstd = p.c("act", rstd_f, waits=[e_stat])
                if B2LV < 4:
                    e_app = e_rstd
                    continue

                qb = AR2[:, 4096:6144] if which == "q" else AR2[:, 6144:8192]
                s1b = s1.rearrange("p (h o) -> p h o", o=1).broadcast_to(
                    [128, 16, 128])
                s2b = s2.rearrange("p (h o) -> p h o", o=1).broadcast_to(
                    [128, 16, 128])
                qb3 = qb.rearrange("p (h j) -> p h j", h=16)

                def app_f(e, qb3=qb3, s1b=s1b, s2b=s2b):
                    e.tensor_tensor(qnh, qnh, s1b, ALU.subtract)
                    return e.tensor_tensor(qb3, qnh, s2b, ALU.mult)
                e_app = p.c("dve", app_f,
                            waits=[e_rstd, e_qb_free.get(which)])
                if B2LV < 5:
                    continue

                stg = AR2[:, 8192:10240] if which == "q" else AR2[:, 10240:12288]
                e_ev = None
                for h in range(16):
                    bank = [PMB, PMB2][h % 2]
                    e_tr = p.c("pe", lambda e, h=h, bank=bank, qb=qb:
                               e.transpose(bank[:, 0:128],
                                           qb[:, h * 128:(h + 1) * 128],
                                           c_ident),
                               waits=[e_app, e_slot.get(h % 2)])
                    e_ev = p.c("act", lambda e, h=h, bank=bank, stg=stg:
                               e.activation(stg[:, h * 128:(h + 1) * 128],
                                            bank[:, 0:128],
                                            AF.Identity, scale=c_lnsc,
                                            bias=c_lnbs),
                               waits=[e_tr, e_qb_free.get(("stg", which))])
                    e_slot[h % 2] = e_ev
                e_qb_free[which] = e_ev
                e_st = p.dma("s",
                             dstD.ap().rearrange("(h p) t -> p h t", p=128)
                             [:, :, tt * 128:(tt + 1) * 128],
                             stg.rearrange("p (h t) -> p h t", h=16),
                             waits=[e_ev])
                e_qb_free[("stg", which)] = e_st
                last = e_st
        return last

    _PHASE = os.environ.get("MK_PHASE", "")
    prologue()
    p.barrier()

    if _PHASE != "pro":
        h_evts = ln_phase(x0, lambda ch, lo, s_: h_ap(ch)[:, lo:lo + s_],
                          False, 0, T)
        if _PHASE != "ln":
            pst = {}
            qk_proj(wqTD, qf_s, h_evts, pst)
            qk_proj(wkTD, kf_s, h_evts, pst)
            v_proj(h_evts, pst)
            p.barrier()
            if _PHASE != "proj":
                attn_core("soft", qf_s, kf_s, vt_s, 0, 1e-8)
                p.barrier()
                if _PHASE != "attn":
                    out_proj(woTD, x0, xs1, None)
                    p.barrier()
    if _PHASE:
        p.barrier()
        p.dma("s", y.ap(), x0.ap())
        p.final_wait()
        p.emit()
        ctx.close()
        return nc

    if stage >= 2:
        ffn(xs1, xs2)
        p.barrier()
    if stage >= 3:
        _BP = os.environ.get("MK_BP", "")
        h_evts = ln_phase(xs2, lambda ch, lo, s_: h_ap(ch)[:, lo:lo + s_],
                          False, 0, T)
        qkv_proj_b(h_evts)
        p.barrier()
        if _BP != "b1":
            rope_ln_b()
            p.barrier()
            if _BP != "b2":
                attn_core("poly", qf_s, kf_s, qkv_s, 4096, 0.0)
                p.barrier()
                if _BP != "b3":
                    out_proj(woutTD, xs2, xs3, None)
                    p.barrier()
        if _BP:
            p.barrier()
            dmp = os.environ.get("MK_DUMP", "")
            if dmp == "tabs":
                p.dma("g", y.ap()[0:128, 0:2048], CON[:, 640:2688])
            elif dmp == "conf":
                p.dma("s", y.ap()[0:128, 0:640], CONF[:, 0:640])
            elif dmp == "qkv":
                p.dma("g", y.ap(), qkv_s.ap()[:, 0:2048])
            elif dmp in ("qf", "kf"):
                p.dma("g", y.ap(), {"qf": qf_s, "kf": kf_s}[dmp].ap())
            else:
                p.dma("s", y.ap(), xs2.ap())
            p.final_wait()
            p.emit()
            ctx.close()
            return nc
    dmp = os.environ.get("MK_DUMP", "")
    if dmp:
        p.barrier()
        if dmp == "qkv":
            p.dma("g", y.ap(), qkv_s.ap()[:, 0:2048])
        elif dmp == "tabs":
            p.dma("g", y.ap()[0:128, 0:2048], CON[:, 640:2688])
        elif dmp == "rope":
            p.dma("s", y.ap()[0:64, 0:2048], AR4[0:64, 0:2048])
            p.dma("s", y.ap()[64:128, 0:2048],
                  AR1[0:64, 24576:32768].bitcast(F32)[0:64, 0:2048])
            p.dma("s", y.ap()[128:192, 0:2048],
                  AR1[0:64, 16384:24576].bitcast(F32)[0:64, 0:2048])
        else:
            t = {"qf": qf_s, "kf": kf_s, "ao": ao_s, "x1": xs1, "x2": xs2,
                 "x3": xs3}[dmp]
            q = "g" if t.dtype == BF16 else "s"
            p.dma(q, y.ap(), t.ap())
    elif stage >= 4:
        ffn(xs3, y)
    else:
        src = {1: xs1, 2: xs2, 3: xs3}[stage]
        p.dma("s", y.ap(), src.ap())
    p.final_wait()
    p.emit()
    ctx.close()
    return nc


def _tile_lhsT(Wt):
    K, M = Wt.shape
    kc, mc = K // 128, M // 128
    return np.ascontiguousarray(
        Wt.reshape(kc, 128, mc, 128).transpose(2, 0, 1, 3))


def _ktab(shift):
    t = np.arange(2048, dtype=np.float64)
    invf = 1.0 / 10000.0 ** (np.arange(0, 128, 2, dtype=np.float64) / 128.0)
    th = invf[:, None] * t[None, :]
    if shift:
        th = th + math.pi / 2
    return np.round(th / (2 * math.pi)).astype(np.float32)


def _prep_inputs(i):
    f32 = np.float32
    H = sylvester(128).astype(f32)
    maskA = np.where(np.arange(128)[None, :] <= np.arange(128)[:, None],
                     0.0, MASK_NEG).astype(f32)
    maskT = np.where(np.arange(128)[None, :] >= np.arange(128)[:, None],
                     1.0, 0.0).astype(ml_dtypes.bfloat16)
    wsu_all = np.asarray(i["L0_ffn_Wscale_ug"], dtype=f32)
    shared = {
        "wqT": _tile_lhsT(np.ascontiguousarray(np.asarray(i["L0_attn_a_wq"], f32).T)),
        "wkT": _tile_lhsT(np.ascontiguousarray(np.asarray(i["L0_attn_a_wk"], f32).T)),
        "wvr": np.ascontiguousarray(np.asarray(i["L0_attn_a_wv"], f32).T).reshape(16, 128, D),
        "woT": _tile_lhsT(np.ascontiguousarray(np.asarray(i["L0_attn_a_wo"], f32).T)),
        "wqkv": np.ascontiguousarray(np.asarray(i["L0_attn_b_w_qkv"], f32)).reshape(16, 128, 3 * D),
        "woutT": _tile_lhsT(np.ascontiguousarray(np.asarray(i["L0_attn_b_w_out"], f32))),
        "wupT": _tile_lhsT(np.ascontiguousarray(np.asarray(i["L0_ffn_W_up"], f32).T)),
        "wgT": _tile_lhsT(np.ascontiguousarray(np.asarray(i["L0_ffn_W_gate"], f32).T)),
        "wdnT": _tile_lhsT(np.ascontiguousarray(np.asarray(i["L0_ffn_W_down"], f32).T)),
        "su_ug": np.ascontiguousarray(np.asarray(i["L0_ffn_SU_ug"], f32).reshape(16, 128).T),
        "su_dp": np.ascontiguousarray(np.asarray(i["L0_ffn_SU_dp"], f32).reshape(64, 128).T),
        "wsu": np.ascontiguousarray(wsu_all[:IN].reshape(64, 128).T),
        "wsg": np.ascontiguousarray(wsu_all[IN:].reshape(64, 128).T),
        "wsd": np.ascontiguousarray(np.asarray(i["L0_ffn_Wscale_dp"], f32).reshape(16, 128).T),
        "lnsc": np.ascontiguousarray(np.asarray(i["L0_attn_b_ln_scale"], f32).reshape(128, 1)),
        "lnbs": np.ascontiguousarray(np.asarray(i["L0_attn_b_ln_bias"], f32).reshape(128, 1)),
        "h128": H.astype(ml_dtypes.bfloat16),
        "identb": np.eye(128, dtype=ml_dtypes.bfloat16),
        "identf": np.eye(128, dtype=f32),
        "maskA": maskA,
        "maskT": maskT,
        "onescol": np.ones((128, 1), dtype=ml_dtypes.bfloat16),
        "onesrow": np.ones((1, 128), dtype=ml_dtypes.bfloat16),
        "onesrow32": np.ones((1, 128), dtype=f32),
        "iota_t": np.arange(T, dtype=f32).reshape(1, T),
        "iota_j": np.arange(128, dtype=f32).reshape(128, 1),
        "iota2d": np.tile(np.arange(T, dtype=f32)[None, :], (64, 1)),
        "k1tab": _ktab(False),
        "k2tab": _ktab(True),
    }
    shared = {k: np.ascontiguousarray(v) for k, v in shared.items()}
    x = np.asarray(i["L0_attn_a_x"], dtype=np.float32)
    in_maps = []
    for c in range(8):
        m = dict(shared)
        m["x0"] = np.ascontiguousarray(x[c].T)
        in_maps.append(m)
    return in_maps


def kernel(**inputs):
    in_maps = _prep_inputs(inputs)
    nc = build_program()
    res = run_bass_kernel_spmd(nc, in_maps, core_ids=list(range(8)))
    out = np.empty((8, T, D), dtype=np.float32)
    for c in range(8):
        out[c] = np.asarray(res.results[c]["y"], dtype=np.float32).T
    return out
